# version 04
# baseline (speedup 1.0000x reference)
import numpy as np
from contextlib import ExitStack

import concourse.bass as bass
import concourse.mybir as mybir
from concourse.bass_utils import run_bass_kernel_spmd

F32 = mybir.dt.float32
AF = mybir.ActivationFunctionType
ALU = mybir.AluOpType
AX = mybir.AxisListType

D = 1024
T = 4096
TT = 512
NTILES = T // TT
DC = D // 128
EPS = 1e-5
LC = 16
NCH = T // LC

ENGINES = ("tensor", "vector", "scalar", "gpsimd", "sync")


class Prog:

    def __init__(self, nc):
        self.nc = nc
        self._es = ExitStack()
        self.sem = {e: self._es.enter_context(nc.semaphore(f"sem_{e}")) for e in ENGINES}
        self.dsem = {e: self._es.enter_context(nc.semaphore(f"dsem_{e}"))
                     for e in ("sync", "scalar", "gpsimd", "vector")}
        self.q = {e: [] for e in ENGINES}
        self.incs = {e: set() for e in ENGINES}
        self.last_w = {}
        self.readers = {}
        self.water = {}

    def _need_wait(self, cons, prod, idx, is_dma):
        if cons == prod and not is_dma:
            return False
        return idx > self.water.get((cons, prod, is_dma), -1)

    def _add_dep(self, cons, waits, prod, idx, is_dma):
        if not self._need_wait(cons, prod, idx, is_dma):
            return
        self.water[(cons, prod, is_dma)] = idx
        if not is_dma:
            self.incs[prod].add(idx)
        waits.append((prod, idx, is_dma))

    def _add_self_dep(self, eng, waits, idx):
        if idx > self.water.get((eng, eng, False), -1):
            self.water[(eng, eng, False)] = idx
            self.incs[eng].add(idx)
            waits.append((eng, idx, False))

    def emit(self, eng, fn, reads=(), writes=(), is_dma=False):
        waits = []
        for k in reads:
            lw = self.last_w.get(k)
            if lw is not None:
                if lw[0] == eng and not lw[2] and not is_dma \
                        and len(self.q[eng]) - lw[1] <= 2:
                    self._add_self_dep(eng, waits, lw[1])
                else:
                    self._add_dep(eng, waits, *lw)
        for k in writes:
            lw = self.last_w.get(k)
            if lw is not None:
                self._add_dep(eng, waits, *lw)
            for rdeng, (ridx, rdma) in self.readers.get(k, {}).items():
                self._add_dep(eng, waits, rdeng, ridx, rdma)
        idx = len(self.q[eng])
        self.q[eng].append({"fn": fn, "waits": waits, "is_dma": is_dma})
        for k in reads:
            self.readers.setdefault(k, {})[eng] = (idx, is_dma)
        for k in writes:
            self.last_w[k] = (eng, idx, is_dma)
            self.readers[k] = {}
        return idx

    def barrier(self):
        cur = {e: len(self.q[e]) - 1 for e in ENGINES}
        for cons in ENGINES:
            waits = []
            for prod in ENGINES:
                if prod == cons or cur[prod] < 0:
                    continue
                j = cur[prod]
                while j >= 0 and self.q[prod][j]["is_dma"]:
                    j -= 1
                if j >= 0:
                    self._add_dep(cons, waits, prod, j, False)
            for prod in ("sync", "scalar", "gpsimd", "vector"):
                ndma = sum(1 for it in self.q[prod] if it["is_dma"])
                if ndma:
                    waits.append((prod, ("dma", ndma), True))
            self.q[cons].append({"fn": None, "waits": waits, "is_dma": False})
        for cons in ENGINES:
            for prod in ENGINES:
                if prod != cons:
                    self.water[(cons, prod, False)] = len(self.q[prod]) - 1
                self.water[(cons, prod, True)] = len(self.q[prod]) - 1

    def final_dma_wait(self):
        waits = []
        for prod in ("sync", "scalar", "gpsimd", "vector"):
            ndma = sum(1 for it in self.q[prod] if it["is_dma"])
            if ndma:
                waits.append((prod, ("dma", ndma), True))
        self.q["sync"].append({"fn": None, "waits": waits, "is_dma": False})

    def finalize(self):
        import bisect
        nc = self.nc
        inc_sorted = {e: sorted(self.incs[e]) for e in ENGINES}
        dma_cum = {}
        for e in ENGINES:
            c, lst = 0, []
            for it in self.q[e]:
                if it["is_dma"]:
                    c += 1
                lst.append(c)
            dma_cum[e] = lst
        resolved = {}
        for e in ENGINES:
            rs = []
            for it in self.q[e]:
                ws = []
                for (prod, idx, is_dma) in it["waits"]:
                    if isinstance(idx, tuple) and idx[0] == "dma":
                        ws.append((self.dsem[prod], 16 * idx[1]))
                    elif is_dma:
                        ws.append((self.dsem[prod], 16 * dma_cum[prod][idx]))
                    else:
                        ws.append((self.sem[prod], bisect.bisect_right(inc_sorted[prod], idx)))
                rs.append(ws)
            resolved[e] = rs

        def replay(e, eng_obj):
            for i, it in enumerate(self.q[e]):
                for (sem, v) in resolved[e][i]:
                    eng_obj.wait_ge(sem, v)
                if it["fn"] is None:
                    if i in self.incs[e]:
                        eng_obj.nop().then_inc(self.sem[e], 1)
                    continue
                inst = it["fn"](eng_obj)
                if it["is_dma"]:
                    inst.then_inc(self.dsem[e], 16)
                elif i in self.incs[e]:
                    inst.then_inc(self.sem[e], 1)

        with nc.Block() as blk:
            @blk.tensor
            def _(t):
                replay("tensor", t)

            @blk.vector
            def _(v):
                replay("vector", v)

            @blk.scalar
            def _(s):
                replay("scalar", s)

            @blk.gpsimd
            def _(g):
                replay("gpsimd", g)

            @blk.sync
            def _(s):
                replay("sync", s)

    def dma(self, queue, out_ap, in_ap, reads=(), writes=()):
        self.emit(queue, lambda e: e.dma_start(out=out_ap, in_=in_ap),
                  reads=reads, writes=writes, is_dma=True)


def build_nc(upto="full"):
    nc = bass.Bass("TRN2", target_bir_lowering=False, detect_race_conditions=False)
    p = Prog(nc)
    _stack = [ExitStack()]

    def sb(name, shape, dtype=F32):
        return _stack[-1].enter_context(nc.sbuf_tensor(name, shape, dtype))

    def psum(name, shape, dtype=F32):
        return _stack[-1].enter_context(nc.psum_tensor(name, shape, dtype))

    def push_scope():
        _stack.append(ExitStack())

    def pop_scope():
        _stack.pop().close()

    def dram_in(name, shape):
        return nc.dram_tensor(name, shape, F32, kind="ExternalInput").ap()

    def scratch(name, shape):
        return nc.dram_tensor(name, shape, F32, kind="Internal").ap()

    xin = dram_in("x_t", [D, T])
    nw_in = dram_in("norm_w", [128, DC])
    nb_in = dram_in("norm_b", [128, DC])
    mamba_w = {}
    for pre in ("fwd", "bwd"):
        mamba_w[pre] = {
            "ipT": dram_in(f"{pre}_ipT", [512, 2048]),
            "opT": dram_in(f"{pre}_opT", [1024, 512]),
            "xpT": dram_in(f"{pre}_xpT", [1024, 64]),
            "dtT": dram_in(f"{pre}_dtT", [32, 1024]),
            "cw": dram_in(f"{pre}_cw", [128, DC * 4]),
            "cb": dram_in(f"{pre}_cb", [128, DC]),
            "dtb": dram_in(f"{pre}_dtb", [128, DC]),
            "alog": dram_in(f"{pre}_alog", [128, DC * 16]),
            "Dv": dram_in(f"{pre}_Dv", [128, DC]),
        }
    wfc_in = dram_in("w_fc", [4096, 1024])
    wpj_in = dram_in("w_proj", [1024, 4096])
    ffuN_in = dram_in("ffn_uN", [4096, 16])
    ffv_in = dram_in("ffn_vT", [1024, 16])
    ffb_in = dram_in("ffn_fbT", [16, 16])
    gate_in = dram_in("ffn_gate", [1, 1])
    xblk_in = dram_in("l1_xblk", [128, 128 * 128])
    zblk_in = dram_in("l1_zblk", [128, 128 * 128])
    bce_in = dram_in("l1_bce", [128, 128 * 128])
    cce_in = dram_in("l1_cce", [128, 128 * 128])
    obd_in = dram_in("l1_obd", [128, 8])
    ident_in = dram_in("ident", [128, 128])
    l1lnw_in = dram_in("l1_lnw", [128, DC])
    l1lnb_in = dram_in("l1_lnb", [128, DC])
    l1dm_in = dram_in("l1_dm", [128, DC])
    woT_in = dram_in("l1_woT", [1024, 1024])
    bo_in = dram_in("l1_bo", [128, DC])
    l1cc_in = dram_in("l1_cc", [128, 128])
    wrT_in = dram_in("l1_wrT", [1024, 1024])
    wkT_in = dram_in("l1_wkT", [1024, 1024])
    wvT_in = dram_in("l1_wvT", [1024, 1024])
    mur_in = dram_in("l1_mur", [128, DC])
    muk_in = dram_in("l1_muk", [128, DC])

    scr_ternc = nc.dram_tensor("scr_ternc", [1024, 4096], mybir.dt.bfloat16, kind="Internal").ap()
    scr_ternp = nc.dram_tensor("scr_ternp", [4096, 1024], mybir.dt.bfloat16, kind="Internal").ap()
    scr_scc = scratch("scr_scc", [128, 32])
    scr_scp = scratch("scr_scp", [128, 8])
    scr_ueff = scratch("scr_ueff", [16, 4096])
    scr_ipW = {pre: scratch(f"scr_ipW_{pre}", [512, 2048]) for pre in ("fwd", "bwd")}
    scr_ipcv = {pre: scratch(f"scr_ipcv_{pre}", [128, 16]) for pre in ("fwd", "bwd")}
    scr_ipcv_row = {pre: scratch(f"scr_ipcvr_{pre}", [1, 2048]) for pre in ("fwd", "bwd")}
    scr_outb = scratch("scr_outb", [512, T])
    scr_x1 = scratch("scr_x1", [D, T])
    scr_xn1 = scratch("scr_xn1", [D, T])
    scr_xcd = scratch("scr_xcd", [D, T])
    scr_kj = scratch("scr_kj", [D, 16])
    scr_yst = scratch("scr_yst", [D, T])

    out_t = nc.dram_tensor("out_t", [D, T], F32, kind="ExternalOutput").ap()
    dbg_xc2 = dbg_carr = None
    if upto in ("p1", "p1x"):
        scr_outb = nc.dram_tensor("dbg_outb", [512, T], F32, kind="ExternalOutput").ap()
    if upto == "p1x":
        dbg_xc2 = nc.dram_tensor("dbg_xc2", [D, T], F32, kind="ExternalOutput").ap()
        dbg_carr = nc.dram_tensor("dbg_carr", [128, NTILES * 128], F32, kind="ExternalOutput").ap()
    if upto == "p2":
        scr_x1 = nc.dram_tensor("dbg_x1", [D, T], F32, kind="ExternalOutput").ap()
        scr_xn1 = nc.dram_tensor("dbg_xn1", [D, T], F32, kind="ExternalOutput").ap()

    ones1 = sb("ones1", [1, 128])
    nones1 = sb("nones1", [1, 128])
    onesc = sb("onesc", [128, 1])
    epsc = sb("epsc", [1, 1])
    ident = sb("identt", [128, 128])
    p.emit("gpsimd", lambda g: g.memset(ones1[:], 1.0), writes=["ones1"])
    p.emit("gpsimd", lambda g: g.memset(nones1[:], -1.0), writes=["nones1"])
    p.emit("gpsimd", lambda g: g.memset(onesc[:], 1.0), writes=["onesc"])
    p.emit("gpsimd", lambda g: g.memset(epsc[:], EPS), writes=["epsc"])
    p.dma("sync", ident[:], ident_in, writes=["identt"])

    push_scope()
    wrow = sb("wrow", [128, 4096])
    wtrn = sb("wtrn", [128, 4096])
    tcol = sb("tcol", [128, 1])
    tcol2 = sb("tcol2", [128, 1])
    wTb = sb("wTb", [128, 128], mybir.dt.bfloat16)
    sccol_c = sb("sccol_c", [128, 32])
    sccol_p = sb("sccol_p", [128, 8])
    gcolB = sb("gcolB", [128, 1])
    ueffT = sb("ueffT", [16, 4096])
    pt0 = psum("pt0", [128, 128])
    ptcv = psum("ptcv", [1, 2048])

    def ternarize(src_ap, nrows_chunks, ncols, dst, sccol, ksc):
        for rc in range(nrows_chunks):
            p.dma("sync", wrow[:, 0:ncols], src_ap[rc * 128:(rc + 1) * 128, :],
                  writes=["wrow"])
            p.emit("vector", lambda v, s=ncols: v.reduce_sum(
                tcol[:], wrow[:, 0:s], axis=AX.X, apply_absolute_value=True),
                reads=["wrow"], writes=["tcol"])
            p.emit("vector", lambda v, s=ncols: v.tensor_scalar(
                tcol[:], tcol[:], 1.0 / s, 1e-5, ALU.mult, ALU.max),
                reads=["tcol"], writes=["tcol"])
            p.emit("vector", lambda v, rc=rc, sccol=sccol: v.tensor_copy(
                sccol[:, rc:rc + 1], tcol[:]),
                reads=["tcol"], writes=[ksc])
            p.emit("vector", lambda v: v.reciprocal(tcol2[:], tcol[:]),
                   reads=["tcol"], writes=["tcol2"])
            p.emit("vector", lambda v, s=ncols: v.tensor_scalar(
                wtrn[:, 0:s], wrow[:, 0:s], tcol2[:, 0:1], None, ALU.mult),
                reads=["wrow", "tcol2"], writes=["wtrn"])
            p.emit("scalar", lambda s_, s=ncols: s_.activation(
                wrow[:, 0:s], wtrn[:, 0:s], AF.Square),
                reads=["wtrn"], writes=["wrow"])
            p.emit("vector", lambda v, s=ncols: v.tensor_single_scalar(
                wrow[:, 0:s], wrow[:, 0:s], 0.25, ALU.is_ge),
                reads=["wrow"], writes=["wrow"])
            p.emit("scalar", lambda s_, s=ncols: s_.activation(
                wtrn[:, 0:s], wtrn[:, 0:s], AF.Sign),
                reads=["wtrn"], writes=["wtrn"])
            p.emit("vector", lambda v, s=ncols: v.tensor_mul(
                wtrn[:, 0:s], wtrn[:, 0:s], wrow[:, 0:s]),
                reads=["wtrn", "wrow"], writes=["wtrn"])
            for cc in range(ncols // 128):
                p.emit("tensor", lambda t, c=cc: t.transpose(
                    pt0[:], wtrn[:, c * 128:(c + 1) * 128], ident[:]),
                    reads=["wtrn", "identt"], writes=["pt0"])
                p.emit("scalar", lambda s_: s_.copy(wTb[:], pt0[:]),
                       reads=["pt0"], writes=["wTb"])
                p.dma("sync", dst[cc * 128:(cc + 1) * 128, rc * 128:(rc + 1) * 128],
                      wTb[:], reads=["wTb"])

    ternarize(wfc_in, 32, 1024, scr_ternc, sccol_c, "sccol_c")
    ternarize(wpj_in, 8, 4096, scr_ternp, sccol_p, "sccol_p")
    p.dma("sync", scr_scc, sccol_c[:], reads=["sccol_c"], writes=["scr_scc"])
    p.dma("sync", scr_scp, sccol_p[:], reads=["sccol_p"], writes=["scr_scp"])
    p.dma("sync", tcol[0:1, 0:1], gate_in, writes=["tcol"])
    p.emit("tensor", lambda t: t.matmul(pt0[:, 0:1], ones1[:], tcol[0:1, 0:1],
                                        start=True, stop=True),
           reads=["tcol", "ones1"], writes=["pt0"])
    p.emit("scalar", lambda s: s.copy(gcolB[:], pt0[:, 0:1]),
           reads=["pt0"], writes=["gcolB"])
    for rc in range(32):
        p.dma("sync", wrow[:, 0:16], ffuN_in[rc * 128:(rc + 1) * 128, :],
              writes=["wrow"])
        p.emit("vector", lambda v, rc=rc: v.reciprocal(
            tcol2[:], sccol_c[:, rc:rc + 1]),
            reads=["sccol_c"], writes=["tcol2"])
        p.emit("vector", lambda v: v.tensor_mul(tcol2[:], tcol2[:], gcolB[:]),
               reads=["tcol2", "gcolB"], writes=["tcol2"])
        p.emit("vector", lambda v: v.tensor_scalar(
            wrow[:, 0:16], wrow[:, 0:16], tcol2[:, 0:1], None, ALU.mult),
            reads=["wrow", "tcol2"], writes=["wrow"])
        p.emit("tensor", lambda t: t.transpose(
            pt0[0:16, 0:128], wrow[:, 0:16], ident[:]),
            reads=["wrow", "identt"], writes=["pt0"])
        p.emit("scalar", lambda s, rc=rc: s.copy(
            ueffT[:, rc * 128:(rc + 1) * 128], pt0[0:16, 0:128]),
            reads=["pt0"], writes=["ueffT"])
    p.dma("sync", scr_ueff, ueffT[:], reads=["ueffT"], writes=["scr_ueff"])
    nwc0 = sb("nwc0", [128, DC])
    nbc0 = sb("nbc0", [128, DC])
    p.dma("sync", nwc0[:], nw_in, writes=["nwc0"])
    p.dma("sync", nbc0[:], nb_in, writes=["nbc0"])
    for pre in ("fwd", "bwd"):
        kcs0 = [0, 1, 2, 3] if pre == "fwd" else [4, 5, 6, 7]
        for i, kc in enumerate(kcs0):
            p.dma("sync", wrow[:, 0:2048], mamba_w[pre]["ipT"][i * 128:(i + 1) * 128, :],
                  writes=["wrow"])
            p.emit("vector", lambda v, kc=kc: v.tensor_scalar(
                wtrn[:, 0:2048], wrow[:, 0:2048], nwc0[:, kc:kc + 1], None, ALU.mult),
                reads=["wrow", "nwc0"], writes=["wtrn"])
            p.dma("sync", scr_ipW[pre][i * 128:(i + 1) * 128, :], wtrn[:, 0:2048],
                  reads=["wtrn"], writes=[f"scr_ipW_{pre}"])
        for i, kc in enumerate(kcs0):
            p.dma("sync", wrow[:, 0:2048], mamba_w[pre]["ipT"][i * 128:(i + 1) * 128, :],
                  writes=["wrow"])
            for ns in range(4):
                p.emit("tensor", lambda t, kc=kc, ns=ns, i=i: t.matmul(
                    ptcv[0:1, ns * 512:(ns + 1) * 512], nbc0[:, kc:kc + 1],
                    wrow[:, ns * 512:(ns + 1) * 512], start=(i == 0), stop=(i == 3)),
                    reads=["wrow", "nbc0"], writes=["ptcv"])
        p.emit("scalar", lambda s, pre=pre: s.copy(wtrn[0:1, 0:2048], ptcv[0:1, 0:2048]),
               reads=["ptcv"], writes=["wtrn"])
        p.dma("sync", scr_ipcv_row[pre], wtrn[0:1, 0:2048], reads=["wtrn"],
              writes=[f"scr_ipcvr_{pre}"])
        p.emit("sync", lambda e, pre=pre: e.dma_start(
            out=wrow[:, 0:16].rearrange("p (c o) -> p c o", o=1),
            in_=scr_ipcv_row[pre].rearrange("one (c p) -> p c one", p=128),
            allow_slow_non_contiguous=True),
            reads=[f"scr_ipcvr_{pre}"], writes=["wrow"], is_dma=True)
        p.dma("sync", scr_ipcv[pre], wrow[:, 0:16], reads=["wrow"],
              writes=[f"scr_ipcv_{pre}"])
    p.barrier()
    pop_scope()

    push_scope()
    xt = sb("xt", [128, DC * TT])
    xn = sb("xn", [128, DC * TT])
    xcr = sb("xcr", [128, DC * (TT + 6)])
    xc2 = sb("xc2", [128, DC * TT])
    sz = sb("sz", [128, DC * TT])
    dtb = sb("dtb", [128, DC * TT])
    dtu = sb("dtu", [128, DC * TT])
    ya = sb("ya", [128, DC * TT])
    arena = sb("arena", [128, 4096])
    xdbl = sb("xdbl", [64, TT])
    carr = sb("carr", [128, 128])
    halo = sb("halo", [128, DC * 3])
    mrow_t = sb("mrow_t", [1, TT])
    vrow_t = sb("vrow_t", [1, TT])
    rstd_t = sb("rstd_t", [1, TT])
    nmr_t = sb("nmr_t", [1, TT])
    mrow, vrow = mrow_t[:], vrow_t[:]
    rstd, nmr = rstd_t[:], nmr_t[:]
    wopS = sb("wopS", [128, 2 * 8 * 128])
    wipS = wopS
    wxp = sb("wxp", [128, DC * 64])
    wdt = sb("wdt", [32, 1024])
    acols = sb("acols", [128, DC * 16])
    cwc = sb("cwc", [128, DC * 4])
    cbc = sb("cbc", [128, DC])
    dtbc = sb("dtbc", [128, DC])
    dvc = sb("dvc", [128, DC])
    wfs = sb("wfs", [128, 2 * 8 * 128], mybir.dt.bfloat16)
    wps = sb("wps", [128, 2 * 8 * 128], mybir.dt.bfloat16)
    xnh = sb("xnh", [128, DC * 256], mybir.dt.bfloat16)
    xnl = sb("xnl", [128, DC * 256], mybir.dt.bfloat16)
    r2h = sb("r2h", [128, 16 * 256], mybir.dt.bfloat16)
    r2l = sb("r2l", [128, 16 * 256], mybir.dt.bfloat16)
    sccS = sb("sccS", [128, 32])
    scpS = sb("scpS", [128, 8])
    uTs = sb("uTs", [16, 128])
    vT = sb("vT", [128, DC * 16])
    fbT = sb("fbT", [16, 16])
    lrt = sb("lrt", [16, 512])
    lr1, lr2 = lrt[:, 0:256], lrt[:, 256:512]
    gcol = sb("gcol", [16, 1])

    ps_a = psum("ps_a", [128, 512])
    ps_b = psum("ps_b", [128, 512])
    ps_c = psum("ps_c", [128, 512])
    ps_d = psum("ps_d", [128, 512])
    ps_e = psum("ps_e", [128, 512])
    ps_f = psum("ps_f", [128, 512])
    ps_s1 = psum("ps_s1", [1, 512])
    ps_s2 = psum("ps_s2", [1, 512])


    ipcvS = sb("ipcvS", [128, 16])

    def load_dir_weights(pre):
        w = mamba_w[pre]
        p.dma("sync", ipcvS[:], scr_ipcv[pre], reads=[f"scr_ipcv_{pre}"],
              writes=["ipcvS"])
        p.dma("sync", wxp[:].rearrange("p (c o) -> p c o", o=64),
              w["xpT"].rearrange("(c p) o -> p c o", p=128), writes=["wxp"])
        p.dma("sync", wdt[:], w["dtT"], writes=["wdt"])
        p.dma("sync", acols[:], w["alog"], writes=["acols"])
        p.dma("sync", cwc[:], w["cw"], writes=["cwc"])
        p.dma("sync", cbc[:], w["cb"], writes=["cbc"])
        p.dma("sync", dtbc[:], w["dtb"], writes=["dtbc"])
        p.dma("sync", dvc[:], w["Dv"], writes=["dvc"])
        p.emit("scalar", lambda s: s.activation(acols[:], acols[:], AF.Exp),
               reads=["acols"], writes=["acols"])
        p.emit("vector", lambda v: v.tensor_scalar(acols[:], acols[:], -1.0, None, ALU.mult),
               reads=["acols"], writes=["acols"])

    def ln_stats(src, src_key):
        sqv = wfs[:].bitcast(F32)
        sq = sqv[:, 0:TT]
        for c in range(DC):
            p.emit("scalar", lambda s, c=c, sq=sq: s.activation(
                sq, src[:, c * TT:(c + 1) * TT], AF.Square),
                reads=[src_key], writes=["wfs0", "wfs1"])
            p.emit("tensor", lambda t, c=c: t.matmul(
                ps_s1[:], onesc[:], src[:, c * TT:(c + 1) * TT],
                start=(c == 0), stop=(c == DC - 1)),
                reads=[src_key, "onesc"], writes=["ps_s1"])
            p.emit("tensor", lambda t, c=c, sq=sq: t.matmul(
                ps_s2[:], onesc[:], sq, start=(c == 0), stop=(c == DC - 1)),
                reads=["wfs0", "wfs1", "onesc"], writes=["ps_s2"])
        p.emit("scalar", lambda s: s.activation(mrow, ps_s1[:], AF.Identity,
                                                scale=1.0 / D),
               reads=["ps_s1"], writes=["rows"])
        p.emit("scalar", lambda s: s.activation(vrow, ps_s2[:], AF.Identity,
                                                scale=1.0 / D),
               reads=["ps_s2"], writes=["rows"])
        srow2 = wps[:].bitcast(F32)[0:1, 512:1024]
        p.emit("scalar", lambda s, srow2=srow2: s.activation(srow2, mrow, AF.Square),
               reads=["rows"], writes=["wps0", "wps1"])
        p.emit("gpsimd", lambda g, srow2=srow2: g.tensor_sub(vrow, vrow, srow2),
               reads=["rows", "wps0", "wps1"], writes=["rows"])
        p.emit("scalar", lambda s: s.activation(rstd, vrow, AF.Ln,
                                                bias=epsc[0:1, 0:1]),
               reads=["rows", "epsc"], writes=["rstd_t"])
        p.emit("scalar", lambda s: s.activation(rstd, rstd, AF.Exp, scale=-0.5),
               reads=["rstd_t"], writes=["rstd_t"])
        p.emit("gpsimd", lambda g: g.tensor_mul(nmr, mrow, rstd),
               reads=["rows", "rstd_t"], writes=["nmr_t"])
        p.emit("tensor", lambda t: t.matmul(ps_c[:], ones1[:], rstd, start=True, stop=True),
               reads=["rstd_t", "ones1"], writes=["ps_c"])
        p.emit("tensor", lambda t: t.matmul(ps_d[:], nones1[:], nmr, start=True, stop=True),
               reads=["nmr_t", "nones1"], writes=["ps_d"])
        p.emit("scalar", lambda s, sqv=sqv: s.copy(sqv[:, 512:1024], ps_c[:]),
               reads=["ps_c"], writes=["wfs0", "wfs1"])
        p.emit("scalar", lambda s: s.copy(wps[:].bitcast(F32)[:, 0:512], ps_d[:]),
               reads=["ps_d"], writes=["wps0", "wps1"])

    def apply_ln(dst, dst_key, src, src_key):
        rb = wfs[:].bitcast(F32)[:, 512:1024]
        nb = wps[:].bitcast(F32)[:, 0:512]
        for c in range(DC):
            p.emit("gpsimd", lambda g, c=c, rb=rb: g.tensor_mul(
                dst[:, c * TT:(c + 1) * TT], src[:, c * TT:(c + 1) * TT], rb),
                reads=[src_key, "wfs0", "wfs1"], writes=[dst_key])
            p.emit("gpsimd", lambda g, c=c, nb=nb: g.tensor_add(
                dst[:, c * TT:(c + 1) * TT], dst[:, c * TT:(c + 1) * TT], nb),
                reads=[dst_key, "wps0", "wps1"], writes=[dst_key])

    def mamba_tile(pre, rev):
        kcs = [4, 5, 6, 7] if rev else [0, 1, 2, 3]
        w = mamba_w[pre]
        for m in range(16):
            bank, bkey = (ps_a, "ps_a") if (m % 2 == 0) else (ps_b, "ps_b")
            buf = m % 2
            p.dma("sync", wipS[:, buf * 1024:buf * 1024 + 512].rearrange("p (c o) -> p c o", o=128),
                  scr_ipW[pre].rearrange("(c p) o -> p c o", p=128)[:, :, m * 128:(m + 1) * 128],
                  reads=[f"scr_ipW_{pre}"], writes=[f"wopS{buf}"])
            for i, kc in enumerate(kcs):
                p.emit("tensor", lambda t, kc=kc, i=i, buf=buf, bank=bank: t.matmul(
                    bank[:], wipS[:, buf * 1024 + i * 128:buf * 1024 + (i + 1) * 128],
                    xn[:, kc * TT:(kc + 1) * TT], start=(i == 0), stop=(i == 3)),
                    reads=[f"wopS{buf}", "xn"], writes=[bkey])
            if m < 8:
                p.emit("scalar", lambda s, m=m, bank=bank: s.activation(
                    xcr[:, m * (TT + 6) + 3:m * (TT + 6) + 3 + TT], bank[:],
                    AF.Identity, bias=ipcvS[:, m:m + 1]),
                    reads=[bkey, "ipcvS"], writes=["xcr"])
            else:
                p.emit("scalar", lambda s, m=m, bank=bank: s.activation(
                    sz[:, (m - 8) * TT:(m - 7) * TT], bank[:], AF.Silu,
                    bias=ipcvS[:, m:m + 1]),
                    reads=[bkey, "ipcvS"], writes=["sz"])
        for c in range(DC):
            base = c * (TT + 6)
            if rev:
                p.emit("vector", lambda v, base=base, c=c: v.tensor_copy(
                    xcr[:, base + 3 + TT:base + 6 + TT], halo[:, c * 3:(c + 1) * 3]),
                    reads=["halo"], writes=["xcr"])
            else:
                p.emit("vector", lambda v, base=base, c=c: v.tensor_copy(
                    xcr[:, base:base + 3], halo[:, c * 3:(c + 1) * 3]),
                    reads=["halo"], writes=["xcr"])
        for c in range(DC):
            base = c * (TT + 6)
            if rev:
                p.emit("vector", lambda v, base=base, c=c: v.tensor_copy(
                    halo[:, c * 3:(c + 1) * 3], xcr[:, base + 3:base + 6]),
                    reads=["xcr"], writes=["halo"])
            else:
                p.emit("vector", lambda v, base=base, c=c: v.tensor_copy(
                    halo[:, c * 3:(c + 1) * 3], xcr[:, base + TT:base + TT + 3]),
                    reads=["xcr"], writes=["halo"])
        for c in range(DC):
            base = c * (TT + 6) + 3
            offs = [(3, 0), (2, 1), (1, 2), (0, 3)] if rev else [(-3, 0), (-2, 1), (-1, 2), (0, 3)]
            o0, j0 = offs[0]
            p.emit("vector", lambda v, c=c, base=base, o0=o0, j0=j0: v.tensor_scalar(
                xc2[:, c * TT:(c + 1) * TT], xcr[:, base + o0:base + o0 + TT],
                cwc[:, c * 4 + j0:c * 4 + j0 + 1], None, ALU.mult),
                reads=["xcr", "cwc"], writes=["xc2"])
            for (o, j) in offs[1:]:
                p.emit("vector", lambda v, c=c, base=base, o=o, j=j: v.scalar_tensor_tensor(
                    xc2[:, c * TT:(c + 1) * TT], xcr[:, base + o:base + o + TT],
                    cwc[:, c * 4 + j:c * 4 + j + 1], xc2[:, c * TT:(c + 1) * TT],
                    ALU.mult, ALU.add),
                    reads=["xcr", "cwc", "xc2"], writes=["xc2"])
            p.emit("scalar", lambda s, c=c: s.activation(
                xc2[:, c * TT:(c + 1) * TT], xc2[:, c * TT:(c + 1) * TT], AF.Silu,
                bias=cbc[:, c:c + 1]),
                reads=["xc2", "cbc"], writes=["xc2"])
        for c in range(DC):
            p.emit("tensor", lambda t, c=c: t.matmul(
                ps_e[0:64, :], wxp[:, c * 64:(c + 1) * 64], xc2[:, c * TT:(c + 1) * TT],
                start=(c == 0), stop=(c == DC - 1)),
                reads=["wxp", "xc2"], writes=["ps_e"])
        p.emit("scalar", lambda s: s.copy(xdbl[:], ps_e[0:64, :]),
               reads=["ps_e"], writes=["xdbl"])
        for c in range(DC):
            bank, bkey = (ps_a, "ps_a") if (c % 2 == 0) else (ps_b, "ps_b")
            p.emit("tensor", lambda t, c=c, bank=bank: t.matmul(
                bank[:], wdt[:, c * 128:(c + 1) * 128], xdbl[0:32, :],
                start=True, stop=True),
                reads=["wdt", "xdbl"], writes=[bkey])
            p.emit("scalar", lambda s, c=c, bank=bank: s.activation(
                dtb[:, c * TT:(c + 1) * TT], bank[:], AF.Exp, bias=dtbc[:, c:c + 1]),
                reads=[bkey, "dtbc"], writes=["dtb"])
            p.emit("scalar", lambda s, c=c: s.activation(
                dtb[:, c * TT:(c + 1) * TT], dtb[:, c * TT:(c + 1) * TT], AF.Ln, bias=1.0),
                reads=["dtb"], writes=["dtb"])
        for c in range(DC):
            p.emit("vector", lambda v, c=c: v.tensor_mul(
                dtu[:, c * TT:(c + 1) * TT], dtb[:, c * TT:(c + 1) * TT],
                xc2[:, c * TT:(c + 1) * TT]),
                reads=["dtb", "xc2"], writes=["dtu"])
        dA_sets = [arena[:, 0:2048], r2h[:].bitcast(F32)]
        dB_sets = [arena[:, 2048:4096], r2l[:].bitcast(F32)]
        kA_sets = ["arena", "r2h"]
        kB_sets = ["arena2", "r2l"]
        for n in range(16):
            bb, bbk = (ps_c, "ps_c") if (n % 2 == 0) else (ps_e, "ps_e")
            cbn, cbk = (ps_d, "ps_d") if (n % 2 == 0) else (ps_f, "ps_f")
            selB = ident[0:64, 32 + n:33 + n].broadcast_to([64, 128])
            selC = ident[0:64, 48 + n:49 + n].broadcast_to([64, 128])
            p.emit("tensor", lambda t, selB=selB, bb=bb: t.matmul(
                bb[:], selB, xdbl[0:64, :], start=True, stop=True),
                reads=["xdbl", "identt"], writes=[bbk])
            p.emit("tensor", lambda t, selC=selC, cbn=cbn: t.matmul(
                cbn[:], selC, xdbl[0:64, :], start=True, stop=True),
                reads=["xdbl", "identt"], writes=[cbk])
            bco = (n % 2) * TT
            p.emit("scalar", lambda s, bb=bb, bco=bco: s.copy(
                xcr[:, bco:bco + TT], bb[:]),
                reads=[bbk], writes=["xcr"])
            for ch in range(2):
                q = (n * 2 + ch) % 2
                dA, dB = dA_sets[q], dB_sets[q]
                kA, kB = kA_sets[q], kB_sets[q]
                c0 = ch * 4
                for ci in range(4):
                    c = c0 + ci
                    p.emit("scalar", lambda s, dA=dA, c=c, ci=ci, n=n: s.activation(
                        dA[:, ci * TT:(ci + 1) * TT], dtb[:, c * TT:(c + 1) * TT],
                        AF.Exp, scale=acols[:, c * 16 + n:c * 16 + n + 1]),
                        reads=["dtb", "acols"], writes=[kA])
                bbc = xcr[:, bco:bco + TT].unsqueeze(1).broadcast_to([128, 4, TT])
                p.emit("gpsimd", lambda gg, dB=dB, c0=c0, bbc=bbc: gg.tensor_tensor(
                    dB[:].rearrange("p (c t) -> p c t", t=TT),
                    dtu[:, c0 * TT:(c0 + 4) * TT].rearrange("p (c t) -> p c t", t=TT),
                    bbc, ALU.mult),
                    reads=["dtu", "xcr"], writes=[kB])
                for ci in range(4):
                    c = c0 + ci
                    col = c * 16 + n
                    if rev:
                        d0 = dA[:, ci * TT:(ci + 1) * TT][:, ::-1]
                        d1 = dB[:, ci * TT:(ci + 1) * TT][:, ::-1]
                    else:
                        d0 = dA[:, ci * TT:(ci + 1) * TT]
                        d1 = dB[:, ci * TT:(ci + 1) * TT]
                    p.emit("vector", lambda v, d0=d0, d1=d1, col=col: v.tensor_tensor_scan(
                        d0, d0, d1, carr[:, col:col + 1], ALU.mult, ALU.add),
                        reads=[kA, kB, "carr"], writes=[kA])
                off = 0 if rev else TT - 1
                p.emit("scalar", lambda s, dA=dA, off=off, c0=c0, n=n: s.copy(
                    carr[:].rearrange("p (c n) -> p c n", n=16)[:, c0:c0 + 4, n:n + 1],
                    dA[:].rearrange("p (c t) -> p c t", t=TT)[:, :, off:off + 1]),
                    reads=[kA], writes=["carr"])
                cbc_ap = cbn[:].unsqueeze(1).broadcast_to([128, 4, TT])
                p.emit("vector", lambda v, dA=dA, dB=dB, cbc_ap=cbc_ap: v.tensor_tensor(
                    dB[:].rearrange("p (c t) -> p c t", t=TT),
                    dA[:].rearrange("p (c t) -> p c t", t=TT),
                    cbc_ap, ALU.mult),
                    reads=[kA, cbk], writes=[kB])
                if n == 0:
                    p.emit("gpsimd", lambda g, dB=dB, c0=c0: g.tensor_copy(
                        ya[:, c0 * TT:(c0 + 4) * TT], dB[:]),
                        reads=[kB], writes=["ya"])
                else:
                    p.emit("gpsimd", lambda g, dB=dB, c0=c0: g.tensor_add(
                        ya[:, c0 * TT:(c0 + 4) * TT], ya[:, c0 * TT:(c0 + 4) * TT],
                        dB[:]),
                        reads=["ya", kB], writes=["ya"])
        for c in range(DC):
            p.emit("vector", lambda v, c=c: v.scalar_tensor_tensor(
                dtb[:, c * TT:(c + 1) * TT], xc2[:, c * TT:(c + 1) * TT],
                dvc[:, c:c + 1], ya[:, c * TT:(c + 1) * TT], ALU.mult, ALU.add),
                reads=["xc2", "dvc", "ya"], writes=["dtb"])
            p.emit("vector", lambda v, c=c: v.tensor_mul(
                dtb[:, c * TT:(c + 1) * TT], dtb[:, c * TT:(c + 1) * TT],
                sz[:, c * TT:(c + 1) * TT]),
                reads=["dtb", "sz"], writes=["dtb"])
        dst, dstk = sz, "sz"
        moff = 0
        for m in range(4):
            bank, bkey = (ps_a, "ps_a") if (m % 2 == 0) else (ps_b, "ps_b")
            buf = m % 2
            p.dma("sync", wopS[:, buf * 1024:(buf + 1) * 1024].rearrange("p (c o) -> p c o", o=128),
                  w["opT"].rearrange("(c p) o -> p c o", p=128)[:, :, m * 128:(m + 1) * 128],
                  writes=[f"wopS{buf}"])
            for c in range(DC):
                p.emit("tensor", lambda t, m=m, c=c, buf=buf, bank=bank: t.matmul(
                    bank[:], wopS[:, buf * 1024 + c * 128:buf * 1024 + (c + 1) * 128],
                    dtb[:, c * TT:(c + 1) * TT], start=(c == 0), stop=(c == DC - 1)),
                    reads=[f"wopS{buf}", "dtb"], writes=[bkey])
            p.emit("scalar", lambda s, m=m, bank=bank: s.copy(
                dst[:, (moff + m) * TT:(moff + m + 1) * TT], bank[:]),
                reads=[bkey], writes=[dstk])

    def load_x_tile(ti):
        t0 = ti * TT
        src = xin.rearrange("(c p) t -> p c t", p=128)[:, :, t0:t0 + TT]
        p.dma("sync", xt[:].rearrange("p (c t) -> p c t", t=TT), src, writes=["xt"])

    load_dir_weights("bwd")
    p.emit("gpsimd", lambda g: g.memset(carr[:], 0.0), writes=["carr"])
    p.emit("gpsimd", lambda g: g.memset(halo[:], 0.0), writes=["halo"])
    for ti in range(NTILES - 1, -1, -1):
        load_x_tile(ti)
        ln_stats(xt, "xt")
        apply_ln(xn, "xn", xt, "xt")
        mamba_tile("bwd", rev=True)
        t0 = ti * TT
        p.dma("sync", scr_outb.rearrange("(c p) t -> p c t", p=128)[:, :, t0:t0 + TT],
              sz[:, 0:4 * TT].rearrange("p (c t) -> p c t", t=TT), reads=["sz"])
        if upto == "p1x":
            p.dma("sync", dbg_xc2.rearrange("(c p) t -> p c t", p=128)[:, :, t0:t0 + TT],
                  xc2[:].rearrange("p (c t) -> p c t", t=TT), reads=["xc2"])
            p.dma("sync", dbg_carr[:, ti * 128:(ti + 1) * 128], carr[:], reads=["carr"])
    if upto in ("p1", "p1x"):
        p.final_dma_wait()
        p.finalize()
        return nc

    load_dir_weights("fwd")
    p.emit("gpsimd", lambda g: g.memset(carr[:], 0.0), writes=["carr"])
    p.emit("gpsimd", lambda g: g.memset(halo[:], 0.0), writes=["halo"])
    p.dma("sync", vT[:].rearrange("p (c o) -> p c o", o=16),
          ffv_in.rearrange("(c p) o -> p c o", p=128), writes=["vT"])
    p.dma("sync", fbT[:], ffb_in, writes=["fbT"])
    p.dma("sync", sccS[:], scr_scc, reads=["scr_scc"], writes=["sccS"])
    p.dma("sync", scpS[:], scr_scp, reads=["scr_scp"], writes=["scpS"])

    for ti in range(NTILES):
        t0 = ti * TT
        load_x_tile(ti)
        ln_stats(xt, "xt")
        apply_ln(xn, "xn", xt, "xt")
        mamba_tile("fwd", rev=False)
        p.dma("sync", sz[:, 4 * TT:8 * TT].rearrange("p (c t) -> p c t", t=TT),
              scr_outb.rearrange("(c p) t -> p c t", p=128)[:, :, t0:t0 + TT],
              writes=["sz"])
        for sub in range(2):
            s0 = sub * 256
            for c in range(DC):
                p.emit("gpsimd", lambda gg, c=c, s0=s0: gg.tensor_copy(
                    xnh[:, c * 256:(c + 1) * 256], xn[:, c * TT + s0:c * TT + s0 + 256]),
                    reads=["xn"], writes=["xnh"])
                p.emit("gpsimd", lambda gg, c=c, s0=s0: gg.tensor_sub(
                    xnl[:, c * 256:(c + 1) * 256], xn[:, c * TT + s0:c * TT + s0 + 256],
                    xnh[:, c * 256:(c + 1) * 256]),
                    reads=["xn", "xnh"], writes=["xnl"])
            for c in range(DC):
                p.emit("tensor", lambda t, c=c, s0=s0: t.matmul(
                    ps_e[0:16, 0:256], vT[:, c * 16:(c + 1) * 16],
                    xn[:, c * TT + s0:c * TT + s0 + 256],
                    start=(c == 0), stop=(c == DC - 1)),
                    reads=["vT", "xn"], writes=["ps_e"])
            p.emit("scalar", lambda s: s.copy(lr1, ps_e[0:16, 0:256]),
                   reads=["ps_e"], writes=["lrt"])
            p.emit("tensor", lambda t: t.matmul(
                ps_e[0:16, 0:256], fbT[:], lr1, start=True, stop=True),
                reads=["fbT", "lrt"], writes=["ps_e"])
            p.emit("scalar", lambda s: s.copy(lr2, ps_e[0:16, 0:256]),
                   reads=["ps_e"], writes=["lrt"])
            for half in range(2):
                for hc16 in range(16):
                    hc = half * 16 + hc16
                    bank, bkey = (ps_a, "ps_a") if (hc % 2 == 0) else (ps_b, "ps_b")
                    buf = hc % 2
                    p.dma("sync", wfs[:, buf * 1024:(buf + 1) * 1024].rearrange(
                        "p (c o) -> p c o", o=128),
                        scr_ternc.rearrange("(c p) o -> p c o", p=128)[
                            :, :, hc * 128:(hc + 1) * 128],
                        reads=["scr_ternc"], writes=[f"wfs{buf}"])
                    p.dma("scalar", uTs[:],
                          scr_ueff[:, hc * 128:(hc + 1) * 128],
                          reads=["scr_ueff"], writes=["uTs"])
                    for c in range(DC):
                        p.emit("tensor", lambda t, c=c, buf=buf, bank=bank: t.matmul(
                            bank[:, 0:256],
                            wfs[:, buf * 1024 + c * 128:buf * 1024 + (c + 1) * 128],
                            xnh[:, c * 256:(c + 1) * 256],
                            start=(c == 0), stop=False),
                            reads=[f"wfs{buf}", "xnh"], writes=[bkey])
                        p.emit("tensor", lambda t, c=c, buf=buf, bank=bank: t.matmul(
                            bank[:, 0:256],
                            wfs[:, buf * 1024 + c * 128:buf * 1024 + (c + 1) * 128],
                            xnl[:, c * 256:(c + 1) * 256],
                            start=False, stop=False),
                            reads=[f"wfs{buf}", "xnl"], writes=[bkey])
                    p.emit("tensor", lambda t, bank=bank: t.matmul(
                        bank[:, 0:256], uTs[:], lr2,
                        start=False, stop=True),
                        reads=["uTs", "lrt"], writes=[bkey])
                    p.emit("scalar", lambda s, hc16=hc16, hc=hc, bank=bank: s.activation(
                        arena[:, hc16 * 256:(hc16 + 1) * 256], bank[:, 0:256], AF.Relu,
                        scale=sccS[:, hc:hc + 1]),
                        reads=[bkey, "sccS"], writes=["arena"])
                    p.emit("scalar", lambda s, hc16=hc16: s.activation(
                        arena[:, hc16 * 256:(hc16 + 1) * 256],
                        arena[:, hc16 * 256:(hc16 + 1) * 256], AF.Square),
                        reads=["arena"], writes=["arena"])
                    p.emit("scalar", lambda s, hc16=hc16: s.copy(
                        r2h[:, hc16 * 256:(hc16 + 1) * 256],
                        arena[:, hc16 * 256:(hc16 + 1) * 256]),
                        reads=["arena"], writes=["r2h"])
                    p.emit("gpsimd", lambda gg, hc16=hc16: gg.tensor_sub(
                        r2l[:, hc16 * 256:(hc16 + 1) * 256],
                        arena[:, hc16 * 256:(hc16 + 1) * 256],
                        r2h[:, hc16 * 256:(hc16 + 1) * 256]),
                        reads=["arena", "r2h"], writes=["r2l"])
                for wave in range(2):
                    banks = [(ps_c, "ps_c"), (ps_d, "ps_d"), (ps_e, "ps_e"), (ps_f, "ps_f")]
                    for wi, (bank, bkey) in enumerate(banks):
                        dc_ = wave * 4 + wi
                        buf = wi % 2
                        for go in range(2):
                            p.dma("sync", wps[:, buf * 1024:(buf + 1) * 1024].rearrange(
                                "p (c o) -> p c o", o=128),
                                scr_ternp.rearrange("(c p) o -> p c o", p=128)[
                                    :, half * 16 + go * 8:half * 16 + go * 8 + 8,
                                    dc_ * 128:(dc_ + 1) * 128],
                                reads=["scr_ternp"], writes=[f"wps{buf}"])
                            for c8 in range(8):
                                hc16 = go * 8 + c8
                                p.emit("tensor", lambda t, c8=c8, hc16=hc16, buf=buf,
                                       bank=bank: t.matmul(
                                    bank[:, 0:256],
                                    wps[:, buf * 1024 + c8 * 128:buf * 1024 + (c8 + 1) * 128],
                                    r2h[:, hc16 * 256:(hc16 + 1) * 256],
                                    start=(hc16 == 0), stop=False),
                                    reads=[f"wps{buf}", "r2h"], writes=[bkey])
                                p.emit("tensor", lambda t, c8=c8, hc16=hc16, buf=buf,
                                       bank=bank: t.matmul(
                                    bank[:, 0:256],
                                    wps[:, buf * 1024 + c8 * 128:buf * 1024 + (c8 + 1) * 128],
                                    r2l[:, hc16 * 256:(hc16 + 1) * 256],
                                    start=False, stop=(hc16 == 15)),
                                    reads=[f"wps{buf}", "r2l"], writes=[bkey])
                        if half == 0:
                            p.emit("scalar", lambda s, dc_=dc_, bank=bank: s.activation(
                                ya[:, dc_ * 256:(dc_ + 1) * 256], bank[:, 0:256],
                                AF.Identity, scale=scpS[:, dc_:dc_ + 1]),
                                reads=[bkey, "scpS"], writes=["ya"])
                        else:
                            p.emit("vector", lambda v, dc_=dc_, bank=bank:
                                   v.scalar_tensor_tensor(
                                ya[:, dc_ * 256:(dc_ + 1) * 256], bank[:, 0:256],
                                scpS[:, dc_:dc_ + 1], ya[:, dc_ * 256:(dc_ + 1) * 256],
                                ALU.mult, ALU.add),
                                reads=[bkey, "scpS", "ya"], writes=["ya"])
            for dc_ in range(DC):
                p.emit("vector", lambda v, dc_=dc_, s0=s0: v.tensor_add(
                    xn[:, dc_ * TT + s0:dc_ * TT + s0 + 256],
                    xn[:, dc_ * TT + s0:dc_ * TT + s0 + 256],
                    ya[:, dc_ * 256:(dc_ + 1) * 256]),
                    reads=["xn", "ya"], writes=["xn"])
        for c in range(DC):
            p.emit("gpsimd", lambda gg, c=c: gg.tensor_add(
                xn[:, c * TT:(c + 1) * TT], xn[:, c * TT:(c + 1) * TT],
                xt[:, c * TT:(c + 1) * TT]),
                reads=["xn", "xt"], writes=["xn"])
            p.emit("gpsimd", lambda gg, c=c: gg.tensor_add(
                xn[:, c * TT:(c + 1) * TT], xn[:, c * TT:(c + 1) * TT],
                sz[:, c * TT:(c + 1) * TT]),
                reads=["xn", "sz"], writes=["xn"])
        p.dma("sync", scr_x1.rearrange("(c p) t -> p c t", p=128)[:, :, t0:t0 + TT],
              xn[:].rearrange("p (c t) -> p c t", t=TT), reads=["xn"], writes=["scr_x1"])
        ln_stats(xn, "xn")
        apply_ln(xt, "xt", xn, "xn")
        p.dma("sync", scr_xn1.rearrange("(c p) t -> p c t", p=128)[:, :, t0:t0 + TT],
              xt[:].rearrange("p (c t) -> p c t", t=TT), reads=["xt"], writes=["scr_xn1"])
        for c in range(DC):
            p.emit("gpsimd", lambda gg, c=c: gg.tensor_copy(
                xc2[:, c * TT:(c + 1) * TT].rearrange("p (s c2) -> p s c2", s=LC),
                xt[:, c * TT:(c + 1) * TT].rearrange("p (c2 s) -> p s c2", s=LC)),
                reads=["xt"], writes=["xc2"])
        for c in range(DC):
            p.dma("sync",
                  scr_xcd.rearrange("(c p) (s c2) -> p c s c2", p=128, s=LC)[
                      :, c, :, (t0 // LC):(t0 // LC) + TT // LC],
                  xc2[:, c * TT:(c + 1) * TT].rearrange(
                      "p (s c2) -> p s c2", s=LC),
                  reads=["xc2"], writes=["scr_xcd"])

    if upto == "p2":
        p.final_dma_wait()
        p.finalize()
        return nc

    p.barrier()
    pop_scope()
    push_scope()

    NB = 4
    Xc = [sb(f"Xc{i}", [128, 12 * 128]) for i in range(NB)]
    Zc = [sb(f"Zc{i}", [128, 12 * 128]) for i in range(NB)]
    Ub = [sb(f"Ub{i}", [128, 128]) for i in range(NB)]
    PTb = [sb(f"PTb{i}", [128, 128]) for i in range(NB)]
    Pb = [sb(f"Pb{i}", [128, 128]) for i in range(NB)]
    Hb = [sb(f"Hb{i}", [128, 256]) for i in range(NB)]
    wb = [sb(f"wb{i}", [128, 256]) for i in range(NB)]
    xgb = [sb(f"xgb{i}", [128, 256]) for i in range(NB)]
    ysb = [sb(f"ysb{i}", [128, 256]) for i in range(NB)]
    vcb = [sb(f"vcb{i}", [128, 16]) for i in range(NB)]
    ccres = sb("ccres", [128, 128])
    obd_sb = sb("obd_sb", [128, 8])
    kall = sb("kall", [8, 128 * 16])
    ps_gs = [psum(f"ps_gs{i}", [128, 256]) for i in range(NB)]
    ps_g1 = psum("ps_g1", [128, 256])
    ps_g2 = psum("ps_g2", [128, 256])

    p.dma("sync", ccres[:], l1cc_in, writes=["ccres"])
    p.dma("sync", obd_sb[:], obd_in, writes=["obd"])

    for g_ in range(128):
        s_ = g_ % NB
        ps_g0 = ps_gs[s_]
        evac_eng = "scalar" if g_ % 2 == 0 else "vector"
        X, Z, U, PT, Pm, H, W, XG, YS, VC = (Xc[s_], Zc[s_], Ub[s_], PTb[s_],
                                             Pb[s_], Hb[s_], wb[s_], xgb[s_],
                                             ysb[s_], vcb[s_])
        kX, kZ, kU, kPT, kP, kH, kW, kXG, kYS, kVC = (
            f"X{s_}", f"Z{s_}", f"U{s_}", f"PT{s_}", f"P{s_}", f"H{s_}",
            f"W{s_}", f"XG{s_}", f"YS{s_}", f"VC{s_}")
        kPS = f"ps_gs{s_}"
        p.dma("sync", X[:, 0:128], xblk_in[:, g_ * 128:(g_ + 1) * 128], writes=[kX])
        p.dma("sync", Z[:, 0:128], zblk_in[:, g_ * 128:(g_ + 1) * 128], writes=[kZ])
        p.dma("sync", PT[:, 0:128], bce_in[:, g_ * 128:(g_ + 1) * 128], writes=[kPT])
        p.dma("sync", U[:], cce_in[:, g_ * 128:(g_ + 1) * 128], writes=[kU])
        p.dma("sync", XG[:],
              scr_xcd[8 * g_:8 * g_ + 8, :].rearrange("dlt (s c) -> (dlt s) c", c=256),
              reads=["scr_xcd"], writes=[kXG])
        for r in range(11):
            p.emit("tensor", lambda t, X=X, Z=Z, r=r: t.matmul(
                ps_g0[:, 0:128], Z[:, r * 128:(r + 1) * 128],
                X[:, r * 128:(r + 1) * 128], start=True, stop=True),
                reads=[kX, kZ], writes=[kPS])
            p.emit("tensor", lambda t, X=X, Z=Z, r=r: t.matmul(
                ps_g1[:, 0:128], X[:, r * 128:(r + 1) * 128],
                Z[:, r * 128:(r + 1) * 128], start=True, stop=True),
                reads=[kX, kZ], writes=["ps_g1"])
            p.emit(evac_eng, lambda s, _ee=evac_eng, X=X, r=r: (s.copy if _ee == "scalar" else s.tensor_copy)(
                X[:, (r + 1) * 128:(r + 2) * 128], ps_g0[:, 0:128]),
                reads=[kPS], writes=[kX])
            p.emit(evac_eng, lambda s, _ee=evac_eng, Z=Z, r=r: (s.copy if _ee == "scalar" else s.tensor_copy)(
                Z[:, (r + 1) * 128:(r + 2) * 128], ps_g1[:, 0:128]),
                reads=["ps_g1"], writes=[kZ])
        def ucols(buf, j0, j1):
            return buf[:].rearrange("p (d s) -> p d s", d=8)[:, :, j0:j1]
        p.emit("tensor", lambda t, Z=Z, U=U: t.matmul(
            ps_g0[:, 0:8], Z[:, 0:128], ucols(U, 0, 1), start=True, stop=True),
            reads=[kZ, kU], writes=[kPS])
        p.emit(evac_eng, lambda s, _ee=evac_eng, U=U: (s.copy if _ee == "scalar" else s.tensor_copy)(
            ucols(U, 0, 1), ps_g0[:, 0:8].rearrange("p (d s) -> p d s", s=1)),
            reads=[kPS], writes=[kU])
        for r in range(4):
            nj = 2 ** r
            p.emit("tensor", lambda t, Z=Z, U=U, r=r, nj=nj: t.matmul(
                ps_g0[:, 0:8 * nj], Z[:, r * 128:(r + 1) * 128], ucols(U, 0, nj),
                start=True, stop=True),
                reads=[kZ, kU], writes=[kPS])
            p.emit(evac_eng, lambda s, _ee=evac_eng, U=U, nj=nj: (s.copy if _ee == "scalar" else s.tensor_copy)(
                ucols(U, nj, 2 * nj),
                ps_g0[:, 0:8 * nj].rearrange("p (d s) -> p d s", s=nj)),
                reads=[kPS], writes=[kU])
        for r in range(4):
            nj = 2 ** r
            p.emit("tensor", lambda t, X=X, PT=PT, r=r, nj=nj: t.matmul(
                ps_g0[:, 0:8 * nj], X[:, r * 128:(r + 1) * 128],
                ucols(PT, 16 - nj, 16), start=True, stop=True),
                reads=[kX, kPT], writes=[kPS])
            p.emit(evac_eng, lambda s, _ee=evac_eng, PT=PT, nj=nj: (s.copy if _ee == "scalar" else s.tensor_copy)(
                ucols(PT, 16 - 2 * nj, 16 - nj),
                ps_g0[:, 0:8 * nj].rearrange("p (d s) -> p d s", s=nj)),
                reads=[kPS], writes=[kPT])
        p.emit("tensor", lambda t, PT=PT: t.transpose(
            ps_g0[:, 0:128], PT[:, 0:128], ident[:]),
            reads=[kPT, "identt"], writes=[kPS])
        p.emit(evac_eng, lambda s, _ee=evac_eng, Pm=Pm: (s.copy if _ee == "scalar" else s.tensor_copy)(Pm[:], ps_g0[:, 0:128]),
               reads=[kPS], writes=[kP])
        p.emit("tensor", lambda t, Pm=Pm, XG=XG: t.matmul(
            ps_g1[:], Pm[:], XG[:], start=True, stop=True),
            reads=[kP, kXG], writes=["ps_g1"])
        p.emit(evac_eng, lambda s, _ee=evac_eng, W=W: (s.copy if _ee == "scalar" else s.tensor_copy)(W[:], ps_g1[:]),
               reads=["ps_g1"], writes=[kW])
        p.emit("vector", lambda v, PT=PT, VC=VC: v.reduce_sum(
            VC[:], PT[:, 0:128].rearrange("p (d s) -> p s d", d=8),
            axis=AX.X),
            reads=[kPT], writes=[kVC])
        p.emit("vector", lambda v, VC=VC, g_=g_: v.tensor_scalar(
            VC[:], VC[:], ccres[:, g_:g_ + 1], None, ALU.mult),
            reads=[kVC, "ccres"], writes=[kVC])
        p.emit("tensor", lambda t, VC=VC: t.matmul(
            ps_g2[0:8, 0:16], obd_sb[:], VC[:], start=True, stop=True),
            reads=[kVC, "obd"], writes=["ps_g2"])
        p.emit(evac_eng, lambda s, _ee=evac_eng, g_=g_: (s.copy if _ee == "scalar" else s.tensor_copy)(
            kall[:, g_ * 16:(g_ + 1) * 16], ps_g2[0:8, 0:16]),
            reads=["ps_g2"], writes=["kall"])
        p.emit("gpsimd", lambda gg, H=H: gg.memset(H[:, 0:1], 0.0), writes=[kH])
        p.emit("vector", lambda v, H=H, W=W: v.tensor_copy(H[:, 1:256], W[:, 0:255]),
               reads=[kW], writes=[kH])
        for j in range(8):
            sh = 2 ** j
            p.emit("tensor", lambda t, H=H: t.matmul(
                ps_g0[:], ident[:], H[:], start=True, stop=False),
                reads=[kH, "identt"], writes=[kPS])
            p.emit("tensor", lambda t, X=X, H=H, j=j, sh=sh: t.matmul(
                ps_g0[:, sh:256], X[:, (4 + j) * 128:(5 + j) * 128],
                H[:, 0:256 - sh], start=False, stop=True),
                reads=[kX, kH], writes=[kPS])
            p.emit(evac_eng, lambda s, _ee=evac_eng, H=H: (s.copy if _ee == "scalar" else s.tensor_copy)(H[:], ps_g0[:]),
                   reads=[kPS], writes=[kH])
        p.emit("tensor", lambda t, U=U, H=H: t.matmul(
            ps_g1[:], U[:], H[:], start=True, stop=True),
            reads=[kU, kH], writes=["ps_g1"])
        p.emit(evac_eng, lambda s, _ee=evac_eng, YS=YS: (s.copy if _ee == "scalar" else s.tensor_copy)(YS[:], ps_g1[:]),
               reads=["ps_g1"], writes=[kYS])
        p.dma("sync",
              scr_yst[8 * g_:8 * g_ + 8, :].rearrange("dlt (s c) -> (dlt s) c", c=256),
              YS[:], reads=[kYS], writes=["scr_yst"])
    p.dma("sync",
          scr_kj.rearrange("(gg dlt) j -> dlt gg j", dlt=8),
          kall[:].rearrange("dlt (gg j) -> dlt gg j", j=16),
          reads=["kall"], writes=["scr_kj"])

    p.barrier()
    pop_scope()
    push_scope()

    x1t = sb("x1t", [128, DC * TT])
    xn1t = sb("xn1t", [128, DC * TT])
    yt = sb("yt", [128, DC * TT])
    rbuf = sb("rbuf", [128, DC * TT])
    kact = sb("kact", [128, DC * TT])
    tmpd = sb("tmpd", [128, TT])
    kcols = sb("kcols", [128, DC * 16])
    lnw1 = sb("lnw1", [128, DC])
    lnb1 = sb("lnb1", [128, DC])
    dmc = sb("dmc", [128, DC])
    boc = sb("boc", [128, DC])
    murc = sb("murc", [128, DC])
    mukc = sb("mukc", [128, DC])
    xprevh = sb("xprevh", [128, DC])
    wS = sb("wS", [128, 2 * 8 * 128])
    mrow2 = sb("mrow2", [1, TT])
    vrow2 = sb("vrow2", [1, TT])
    rstd2 = sb("rstd2", [1, TT])
    nmr2 = sb("nmr2", [1, TT])
    srow3 = sb("srow3", [1, TT])
    pe_a = psum("pe_a", [128, 512])
    pe_b = psum("pe_b", [128, 512])
    pe_c = psum("pe_c", [128, 512])
    pe_d = psum("pe_d", [128, 512])
    pe_s1 = psum("pe_s1", [1, 512])
    pe_s2 = psum("pe_s2", [1, 512])

    p.dma("sync", kcols[:].rearrange("p (c j) -> p c j", j=16),
          scr_kj.rearrange("(c p) j -> p c j", p=128), reads=["scr_kj"], writes=["kcols"])
    p.dma("sync", lnw1[:], l1lnw_in, writes=["lnw1"])
    p.dma("sync", lnb1[:], l1lnb_in, writes=["lnb1"])
    p.dma("sync", dmc[:], l1dm_in, writes=["dmc"])
    p.dma("sync", boc[:], bo_in, writes=["boc"])
    p.dma("sync", murc[:], mur_in, writes=["murc"])
    p.dma("sync", mukc[:], muk_in, writes=["mukc"])
    p.emit("gpsimd", lambda gg: gg.memset(xprevh[:], 0.0), writes=["xprevh"])

    def stats2(src, src_key):
        sq = tmpd[:]
        for c in range(DC):
            p.emit("scalar", lambda s, c=c: s.activation(
                sq, src[:, c * TT:(c + 1) * TT], AF.Square),
                reads=[src_key], writes=["tmpd"])
            p.emit("tensor", lambda t, c=c: t.matmul(
                pe_s1[:], onesc[:], src[:, c * TT:(c + 1) * TT],
                start=(c == 0), stop=(c == DC - 1)),
                reads=[src_key, "onesc"], writes=["pe_s1"])
            p.emit("tensor", lambda t, c=c: t.matmul(
                pe_s2[:], onesc[:], sq, start=(c == 0), stop=(c == DC - 1)),
                reads=["tmpd", "onesc"], writes=["pe_s2"])
        p.emit("vector", lambda v: v.tensor_scalar(
            mrow2[:], pe_s1[:], 1.0 / D, None, ALU.mult), reads=["pe_s1"], writes=["r2w"])
        p.emit("vector", lambda v: v.tensor_scalar(
            vrow2[:], pe_s2[:], 1.0 / D, EPS, ALU.mult, ALU.add),
            reads=["pe_s2"], writes=["r2w"])
        p.emit("scalar", lambda s: s.activation(srow3[:], mrow2[:], AF.Square),
               reads=["r2w"], writes=["r2s"])
        p.emit("vector", lambda v: v.tensor_sub(vrow2[:], vrow2[:], srow3[:]),
               reads=["r2w", "r2s"], writes=["r2w"])
        p.emit("scalar", lambda s: s.activation(rstd2[:], vrow2[:], AF.Ln),
               reads=["r2w"], writes=["r2r"])
        p.emit("scalar", lambda s: s.activation(rstd2[:], rstd2[:], AF.Exp, scale=-0.5),
               reads=["r2r"], writes=["r2r"])
        p.emit("vector", lambda v: v.scalar_tensor_tensor(
            nmr2[:], mrow2[:], -1.0, rstd2[:], ALU.mult, ALU.mult),
            reads=["r2w", "r2r"], writes=["r2n"])
        p.emit("tensor", lambda t: t.matmul(pe_c[:], ones1[:], rstd2[:],
                                            start=True, stop=True),
               reads=["r2r", "ones1"], writes=["pe_c"])
        p.emit("tensor", lambda t: t.matmul(pe_d[:], ones1[:], nmr2[:],
                                            start=True, stop=True),
               reads=["r2n", "ones1"], writes=["pe_d"])

    def stream_mm(win, dst_psum, dkey, mov, movkey, m):
        buf = m % 2
        p.dma("sync", wS[:, buf * 1024:(buf + 1) * 1024].rearrange(
            "p (c o) -> p c o", o=128),
            win.rearrange("(c p) o -> p c o", p=128)[:, :, m * 128:(m + 1) * 128],
            writes=[f"wS{buf}"])
        for c in range(DC):
            p.emit("tensor", lambda t, c=c, buf=buf, dst_psum=dst_psum, mov=mov: t.matmul(
                dst_psum[:], wS[:, buf * 1024 + c * 128:buf * 1024 + (c + 1) * 128],
                mov[:, c * TT:(c + 1) * TT], start=(c == 0), stop=(c == DC - 1)),
                reads=[f"wS{buf}", movkey], writes=[dkey])

    for ti in range(NTILES):
        t0 = ti * TT
        p.dma("sync", x1t[:].rearrange("p (c t) -> p c t", t=TT),
              scr_x1.rearrange("(c p) t -> p c t", p=128)[:, :, t0:t0 + TT],
              reads=["scr_x1"], writes=["x1t"])
        p.dma("sync", xn1t[:].rearrange("p (c t) -> p c t", t=TT),
              scr_xn1.rearrange("(c p) t -> p c t", p=128)[:, :, t0:t0 + TT],
              reads=["scr_xn1"], writes=["xn1t"])
        for c in range(DC):
            p.emit("vector", lambda v, c=c: v.tensor_scalar(
                yt[:, c * TT:(c + 1) * TT], xn1t[:, c * TT:(c + 1) * TT],
                dmc[:, c:c + 1], None, ALU.mult),
                reads=["xn1t", "dmc"], writes=["yt"])
            for j in range(16):
                nsub = LC - j
                src = xn1t[:, c * TT:(c + 1) * TT].rearrange(
                    "p (c2 s) -> p c2 s", s=LC)[:, :, 0:nsub]
                dst_ap = yt[:, c * TT:(c + 1) * TT].rearrange(
                    "p (c2 s) -> p c2 s", s=LC)[:, :, j:LC]
                kj = c * 16 + (15 - j)
                p.emit("vector", lambda v, src=src, dst_ap=dst_ap, kj=kj:
                       v.scalar_tensor_tensor(
                    dst_ap, src, kcols[:, kj:kj + 1], dst_ap,
                    ALU.mult, ALU.add),
                    reads=["xn1t", "kcols", "yt"], writes=["yt"])
            p.dma("sync", tmpd[:].rearrange("p (s c2) -> p s c2", c2=TT // LC),
                  scr_yst[c * 128:(c + 1) * 128, :].rearrange(
                      "p (s c2) -> p s c2", c2=256)[:, :, t0 // LC:t0 // LC + TT // LC],
                  reads=["scr_yst"], writes=["tmpd"])
            p.emit("vector", lambda v, c=c: v.tensor_add(
                yt[:, c * TT:(c + 1) * TT],
                yt[:, c * TT:(c + 1) * TT],
                tmpd[:].rearrange("p (s c2) -> p c2 s", c2=TT // LC)),
                reads=["yt", "tmpd"], writes=["yt"])
        stats2(yt, "yt")
        for c in range(DC):
            p.emit("vector", lambda v, c=c: v.tensor_mul(
                yt[:, c * TT:(c + 1) * TT], yt[:, c * TT:(c + 1) * TT], pe_c[:]),
                reads=["yt", "pe_c"], writes=["yt"])
            p.emit("vector", lambda v, c=c: v.tensor_add(
                yt[:, c * TT:(c + 1) * TT], yt[:, c * TT:(c + 1) * TT], pe_d[:]),
                reads=["yt", "pe_d"], writes=["yt"])
            p.emit("vector", lambda v, c=c: v.tensor_scalar(
                yt[:, c * TT:(c + 1) * TT], yt[:, c * TT:(c + 1) * TT],
                lnw1[:, c:c + 1], lnb1[:, c:c + 1], ALU.mult, ALU.add),
                reads=["yt", "lnw1", "lnb1"], writes=["yt"])
            p.emit("scalar", lambda s, c=c: s.activation(
                yt[:, c * TT:(c + 1) * TT], yt[:, c * TT:(c + 1) * TT], AF.Gelu),
                reads=["yt"], writes=["yt"])
        for m in range(DC):
            bank, bkey = (pe_a, "pe_a") if m % 2 == 0 else (pe_b, "pe_b")
            stream_mm(woT_in, bank, bkey, yt, "yt", m)
            p.emit("scalar", lambda s, m=m, bank=bank: s.activation(
                rbuf[:, m * TT:(m + 1) * TT], bank[:], AF.Identity,
                bias=boc[:, m:m + 1]),
                reads=[bkey, "boc"], writes=["rbuf"])
            p.emit("vector", lambda v, m=m: v.tensor_add(
                x1t[:, m * TT:(m + 1) * TT], x1t[:, m * TT:(m + 1) * TT],
                rbuf[:, m * TT:(m + 1) * TT]),
                reads=["x1t", "rbuf"], writes=["x1t"])
            p.emit("vector", lambda v, m=m: v.tensor_add(
                x1t[:, m * TT:(m + 1) * TT], x1t[:, m * TT:(m + 1) * TT],
                xn1t[:, m * TT:(m + 1) * TT]),
                reads=["x1t", "xn1t"], writes=["x1t"])
        for c in range(DC):
            p.emit("vector", lambda v, c=c: v.tensor_copy(
                tmpd[:, 1:TT], xn1t[:, c * TT:(c + 1) * TT - 1]),
                reads=["xn1t"], writes=["tmpd"])
            p.emit("vector", lambda v, c=c: v.tensor_copy(
                tmpd[:, 0:1], xprevh[:, c:c + 1]),
                reads=["xprevh"], writes=["tmpd"])
            p.emit("vector", lambda v, c=c: v.tensor_copy(
                xprevh[:, c:c + 1], xn1t[:, (c + 1) * TT - 1:(c + 1) * TT]),
                reads=["xn1t"], writes=["xprevh"])
            p.emit("vector", lambda v, c=c: v.tensor_sub(
                yt[:, c * TT:(c + 1) * TT], xn1t[:, c * TT:(c + 1) * TT], tmpd[:]),
                reads=["xn1t", "tmpd"], writes=["yt"])
            p.emit("vector", lambda v, c=c: v.scalar_tensor_tensor(
                kact[:, c * TT:(c + 1) * TT], yt[:, c * TT:(c + 1) * TT],
                mukc[:, c:c + 1], tmpd[:], ALU.mult, ALU.add),
                reads=["yt", "mukc", "tmpd"], writes=["kact"])
            p.emit("vector", lambda v, c=c: v.scalar_tensor_tensor(
                yt[:, c * TT:(c + 1) * TT], yt[:, c * TT:(c + 1) * TT],
                murc[:, c:c + 1], tmpd[:], ALU.mult, ALU.add),
                reads=["yt", "murc", "tmpd"], writes=["yt"])
        for m in range(DC):
            bank, bkey = (pe_a, "pe_a") if m % 2 == 0 else (pe_b, "pe_b")
            stream_mm(wrT_in, bank, bkey, yt, "yt", m)
            p.emit("scalar", lambda s, m=m, bank=bank: s.activation(
                rbuf[:, m * TT:(m + 1) * TT], bank[:], AF.Sigmoid),
                reads=[bkey], writes=["rbuf"])
        for m in range(DC):
            bank, bkey = (pe_a, "pe_a") if m % 2 == 0 else (pe_b, "pe_b")
            stream_mm(wkT_in, bank, bkey, kact, "kact", m)
            p.emit("scalar", lambda s, m=m, bank=bank: s.activation(
                yt[:, m * TT:(m + 1) * TT], bank[:], AF.Relu),
                reads=[bkey], writes=["yt"])
            p.emit("scalar", lambda s, m=m: s.activation(
                yt[:, m * TT:(m + 1) * TT], yt[:, m * TT:(m + 1) * TT], AF.Square),
                reads=["yt"], writes=["yt"])
        for m in range(DC):
            bank, bkey = (pe_a, "pe_a") if m % 2 == 0 else (pe_b, "pe_b")
            stream_mm(wvT_in, bank, bkey, yt, "yt", m)
            p.emit("scalar", lambda s, m=m, bank=bank: s.copy(
                kact[:, m * TT:(m + 1) * TT], bank[:]),
                reads=[bkey], writes=["kact"])
            p.emit("vector", lambda v, m=m: v.tensor_mul(
                kact[:, m * TT:(m + 1) * TT], kact[:, m * TT:(m + 1) * TT],
                rbuf[:, m * TT:(m + 1) * TT]),
                reads=["kact", "rbuf"], writes=["kact"])
            p.emit("vector", lambda v, m=m: v.tensor_add(
                x1t[:, m * TT:(m + 1) * TT], x1t[:, m * TT:(m + 1) * TT],
                kact[:, m * TT:(m + 1) * TT]),
                reads=["x1t", "kact"], writes=["x1t"])
        p.dma("sync", out_t.rearrange("(c p) t -> p c t", p=128)[:, :, t0:t0 + TT],
              x1t[:].rearrange("p (c t) -> p c t", t=TT), reads=["x1t"])

    p.final_dma_wait()
    p.finalize()
    return nc


def _prep_host(inputs):
    g = {}
    x = np.ascontiguousarray(inputs["L0_attn_x"])
    g["x_t_cores"] = [np.ascontiguousarray(x[b].T) for b in range(8)]

    def col(v):
        return np.ascontiguousarray(np.asarray(v).reshape(-1, 1))

    def chunked(v):
        return np.ascontiguousarray(np.asarray(v).reshape(DC, 128).T)

    g["norm_w"] = chunked(inputs["L0_attn_norm_w"])
    g["norm_b"] = chunked(inputs["L0_attn_norm_b"])
    for pre in ("fwd", "bwd"):
        P_ = f"L0_attn_{pre}_"
        g[f"{pre}_ipT"] = np.ascontiguousarray(inputs[P_ + "in_proj_w"].T)
        g[f"{pre}_opT"] = np.ascontiguousarray(inputs[P_ + "out_proj_w"].T)
        g[f"{pre}_xpT"] = np.ascontiguousarray(inputs[P_ + "x_proj_w"].T)
        g[f"{pre}_dtT"] = np.ascontiguousarray(inputs[P_ + "dt_proj_w"].T)
        cw = inputs[P_ + "conv_w"][:, 0, :]
        g[f"{pre}_cw"] = np.ascontiguousarray(
            cw.reshape(DC, 128, 4).transpose(1, 0, 2).reshape(128, DC * 4))
        g[f"{pre}_cb"] = chunked(inputs[P_ + "conv_b"])
        g[f"{pre}_dtb"] = chunked(inputs[P_ + "dt_proj_b"])
        al = inputs[P_ + "A_log"]
        g[f"{pre}_alog"] = np.ascontiguousarray(
            al.reshape(DC, 128, 16).transpose(1, 0, 2).reshape(128, DC * 16))
        g[f"{pre}_Dv"] = chunked(inputs[P_ + "D"])
    g["w_fc"] = np.ascontiguousarray(inputs["L0_ffn_w_fc"])
    g["w_proj"] = np.ascontiguousarray(inputs["L0_ffn_w_proj"])
    g["ffn_uN"] = np.ascontiguousarray(inputs["L0_ffn_u"])
    g["ffn_vT"] = np.ascontiguousarray(inputs["L0_ffn_v"].T)
    g["ffn_fbT"] = np.ascontiguousarray(inputs["L0_ffn_fast_b"].T)
    g["ffn_gate"] = np.ascontiguousarray(np.asarray(inputs["L0_ffn_gate"]).reshape(1, 1))
    A = np.asarray(inputs["L1_attn_A"])
    Ag = A.reshape(128, 8, 16, 16)
    xblk = np.zeros((128, 128, 128), np.float32)
    zblk = np.zeros((128, 128, 128), np.float32)
    for dlt in range(8):
        sl = slice(dlt * 16, dlt * 16 + 16)
        xblk[:, sl, sl] = Ag[:, dlt].transpose(0, 2, 1)
        zblk[:, sl, sl] = Ag[:, dlt]
    g["l1_xblk"] = np.ascontiguousarray(xblk.transpose(1, 0, 2).reshape(128, 128 * 128))
    g["l1_zblk"] = np.ascontiguousarray(zblk.transpose(1, 0, 2).reshape(128, 128 * 128))
    Bm = np.asarray(inputs["L1_attn_Bm"]).reshape(128, 8, 16)
    Cm = np.asarray(inputs["L1_attn_Cm"]).reshape(128, 8, 16)
    bce = np.zeros((128, 128, 128), np.float32)
    cce = np.zeros((128, 128, 128), np.float32)
    for dlt in range(8):
        for n_ in range(16):
            bce[:, dlt * 16 + n_, dlt * 16 + 15] = Bm[:, dlt, n_]
            cce[:, dlt * 16 + n_, dlt * 16 + 0] = Cm[:, dlt, n_]
    g["l1_bce"] = np.ascontiguousarray(bce.transpose(1, 0, 2).reshape(128, 128 * 128))
    g["l1_cce"] = np.ascontiguousarray(cce.transpose(1, 0, 2).reshape(128, 128 * 128))
    obd = np.zeros((128, 8), np.float32)
    for dlt in range(8):
        obd[dlt * 16:dlt * 16 + 16, dlt] = 1.0
    g["l1_obd"] = obd
    g["ident"] = np.eye(128, dtype=np.float32)
    g["l1_lnw"] = chunked(inputs["L1_attn_ln_w"])
    g["l1_lnb"] = chunked(inputs["L1_attn_ln_b"])
    g["l1_dm"] = chunked(inputs["L1_attn_Dm"])
    g["l1_woT"] = np.ascontiguousarray(inputs["L1_attn_W_out"].T)
    g["l1_bo"] = chunked(inputs["L1_attn_b_out"])
    g["l1_cc"] = np.ascontiguousarray(Cm.transpose(1, 2, 0).reshape(128, 128))
    g["l1_wrT"] = np.ascontiguousarray(inputs["L1_ffn_W_r"].T)
    g["l1_wkT"] = np.ascontiguousarray(inputs["L1_ffn_W_k"].T)
    g["l1_wvT"] = np.ascontiguousarray(inputs["L1_ffn_W_v"].T)
    g["l1_mur"] = chunked(inputs["L1_ffn_mu_r"])
    g["l1_muk"] = chunked(inputs["L1_ffn_mu_k"])
    return g


def kernel(**inputs):
    g = _prep_host(inputs)
    nc = build_nc("full")
    in_maps = []
    for b in range(8):
        m = {k: v for k, v in g.items() if k != "x_t_cores"}
        m["x_t"] = g["x_t_cores"][b]
        in_maps.append(m)
    res = run_bass_kernel_spmd(nc, in_maps, core_ids=list(range(8)))
    out = np.empty((8, T, D), np.float32)
    for b in range(8):
        out[b] = res.results[b]["out_t"].T
    return out


# version 10
# speedup vs baseline: 1.1646x; 1.1646x over previous
import numpy as np
from contextlib import ExitStack

import concourse.bass as bass
import concourse.mybir as mybir
from concourse.bass_utils import run_bass_kernel_spmd

F32 = mybir.dt.float32
AF = mybir.ActivationFunctionType
ALU = mybir.AluOpType
AX = mybir.AxisListType

D = 1024
T = 4096
TT = 512
NTILES = T // TT
DC = D // 128
EPS = 1e-5
LC = 16
NCH = T // LC

ENGINES = ("tensor", "vector", "scalar", "gpsimd", "sync")


class Prog:

    def __init__(self, nc):
        self.nc = nc
        self._es = ExitStack()
        self.sem = {e: self._es.enter_context(nc.semaphore(f"sem_{e}")) for e in ENGINES}
        self.dsem = {e: self._es.enter_context(nc.semaphore(f"dsem_{e}"))
                     for e in ("sync", "scalar", "gpsimd", "vector")}
        self.q = {e: [] for e in ENGINES}
        self.incs = {e: set() for e in ENGINES}
        self.last_w = {}
        self.readers = {}
        self.water = {}

    def _need_wait(self, cons, prod, idx, is_dma):
        if cons == prod and not is_dma:
            return False
        return idx > self.water.get((cons, prod, is_dma), -1)

    def _add_dep(self, cons, waits, prod, idx, is_dma):
        if not self._need_wait(cons, prod, idx, is_dma):
            return
        self.water[(cons, prod, is_dma)] = idx
        if not is_dma:
            self.incs[prod].add(idx)
        waits.append((prod, idx, is_dma))

    def _add_self_dep(self, eng, waits, idx):
        if idx > self.water.get((eng, eng, False), -1):
            self.water[(eng, eng, False)] = idx
            self.incs[eng].add(idx)
            waits.append((eng, idx, False))

    def emit(self, eng, fn, reads=(), writes=(), is_dma=False):
        waits = []
        for k in reads:
            lw = self.last_w.get(k)
            if lw is not None:
                if lw[0] == eng and not lw[2] and not is_dma \
                        and len(self.q[eng]) - lw[1] <= 2:
                    self._add_self_dep(eng, waits, lw[1])
                else:
                    self._add_dep(eng, waits, *lw)
        for k in writes:
            lw = self.last_w.get(k)
            if lw is not None:
                self._add_dep(eng, waits, *lw)
            for rdeng, (ridx, rdma) in self.readers.get(k, {}).items():
                self._add_dep(eng, waits, rdeng, ridx, rdma)
        idx = len(self.q[eng])
        self.q[eng].append({"fn": fn, "waits": waits, "is_dma": is_dma})
        for k in reads:
            self.readers.setdefault(k, {})[eng] = (idx, is_dma)
        for k in writes:
            self.last_w[k] = (eng, idx, is_dma)
            self.readers[k] = {}
        return idx

    def barrier(self):
        cur = {e: len(self.q[e]) - 1 for e in ENGINES}
        for cons in ENGINES:
            waits = []
            for prod in ENGINES:
                if prod == cons or cur[prod] < 0:
                    continue
                j = cur[prod]
                while j >= 0 and self.q[prod][j]["is_dma"]:
                    j -= 1
                if j >= 0:
                    self._add_dep(cons, waits, prod, j, False)
            for prod in ("sync", "scalar", "gpsimd", "vector"):
                ndma = sum(1 for it in self.q[prod] if it["is_dma"])
                if ndma:
                    waits.append((prod, ("dma", ndma), True))
            self.q[cons].append({"fn": None, "waits": waits, "is_dma": False})
        for cons in ENGINES:
            for prod in ENGINES:
                if prod != cons:
                    self.water[(cons, prod, False)] = len(self.q[prod]) - 1
                self.water[(cons, prod, True)] = len(self.q[prod]) - 1

    def final_dma_wait(self):
        waits = []
        for prod in ("sync", "scalar", "gpsimd", "vector"):
            ndma = sum(1 for it in self.q[prod] if it["is_dma"])
            if ndma:
                waits.append((prod, ("dma", ndma), True))
        self.q["sync"].append({"fn": None, "waits": waits, "is_dma": False})

    def finalize(self):
        import bisect
        nc = self.nc
        inc_sorted = {e: sorted(self.incs[e]) for e in ENGINES}
        dma_cum = {}
        for e in ENGINES:
            c, lst = 0, []
            for it in self.q[e]:
                if it["is_dma"]:
                    c += 1
                lst.append(c)
            dma_cum[e] = lst
        resolved = {}
        for e in ENGINES:
            rs = []
            for it in self.q[e]:
                ws = []
                for (prod, idx, is_dma) in it["waits"]:
                    if isinstance(idx, tuple) and idx[0] == "dma":
                        ws.append((self.dsem[prod], 16 * idx[1]))
                    elif is_dma:
                        ws.append((self.dsem[prod], 16 * dma_cum[prod][idx]))
                    else:
                        ws.append((self.sem[prod], bisect.bisect_right(inc_sorted[prod], idx)))
                rs.append(ws)
            resolved[e] = rs

        def replay(e, eng_obj):
            for i, it in enumerate(self.q[e]):
                for (sem, v) in resolved[e][i]:
                    eng_obj.wait_ge(sem, v)
                if it["fn"] is None:
                    if i in self.incs[e]:
                        eng_obj.nop().then_inc(self.sem[e], 1)
                    continue
                inst = it["fn"](eng_obj)
                if it["is_dma"]:
                    inst.then_inc(self.dsem[e], 16)
                elif i in self.incs[e]:
                    inst.then_inc(self.sem[e], 1)

        with nc.Block() as blk:
            @blk.tensor
            def _(t):
                replay("tensor", t)

            @blk.vector
            def _(v):
                replay("vector", v)

            @blk.scalar
            def _(s):
                replay("scalar", s)

            @blk.gpsimd
            def _(g):
                replay("gpsimd", g)

            @blk.sync
            def _(s):
                replay("sync", s)

    def dma(self, queue, out_ap, in_ap, reads=(), writes=()):
        self.emit(queue, lambda e: e.dma_start(out=out_ap, in_=in_ap),
                  reads=reads, writes=writes, is_dma=True)


def build_nc(upto="full"):
    nc = bass.Bass("TRN2", target_bir_lowering=False, detect_race_conditions=False)
    p = Prog(nc)
    _stack = [ExitStack()]

    def sb(name, shape, dtype=F32):
        return _stack[-1].enter_context(nc.sbuf_tensor(name, shape, dtype))

    def psum(name, shape, dtype=F32):
        return _stack[-1].enter_context(nc.psum_tensor(name, shape, dtype))

    def push_scope():
        _stack.append(ExitStack())

    def pop_scope():
        _stack.pop().close()

    def dram_in(name, shape):
        return nc.dram_tensor(name, shape, F32, kind="ExternalInput").ap()

    def scratch(name, shape):
        return nc.dram_tensor(name, shape, F32, kind="Internal").ap()

    xin = dram_in("x_t", [D, T])
    nw_in = dram_in("norm_w", [128, DC])
    nb_in = dram_in("norm_b", [128, DC])
    mamba_w = {}
    for pre in ("fwd", "bwd"):
        mamba_w[pre] = {
            "ipT": dram_in(f"{pre}_ipT", [512, 2048]),
            "opT": dram_in(f"{pre}_opT", [1024, 512]),
            "xpT": dram_in(f"{pre}_xpT", [1024, 64]),
            "dtT": dram_in(f"{pre}_dtT", [32, 1024]),
            "cw": dram_in(f"{pre}_cw", [128, DC * 4]),
            "cb": dram_in(f"{pre}_cb", [128, DC]),
            "dtb": dram_in(f"{pre}_dtb", [128, DC]),
            "alog": dram_in(f"{pre}_alog", [128, DC * 16]),
            "Dv": dram_in(f"{pre}_Dv", [128, DC]),
        }
    wfc_in = dram_in("w_fc", [4096, 1024])
    wpj_in = dram_in("w_proj", [1024, 4096])
    ffuN_in = dram_in("ffn_uN", [4096, 16])
    ffv_in = dram_in("ffn_vT", [1024, 16])
    ffb_in = dram_in("ffn_fbT", [16, 16])
    gate_in = dram_in("ffn_gate", [1, 1])
    xblk_in = dram_in("l1_xblk", [128, 128 * 128])
    zblk_in = dram_in("l1_zblk", [128, 128 * 128])
    bce_in = dram_in("l1_bce", [128, 128 * 128])
    cce_in = dram_in("l1_cce", [128, 128 * 128])
    obd_in = dram_in("l1_obd", [128, 8])
    ident_in = dram_in("ident", [128, 128])
    l1lnw_in = dram_in("l1_lnw", [128, DC])
    l1lnb_in = dram_in("l1_lnb", [128, DC])
    l1dm_in = dram_in("l1_dm", [128, DC])
    woT_in = dram_in("l1_woT", [1024, 1024])
    bo_in = dram_in("l1_bo", [128, DC])
    l1cc_in = dram_in("l1_cc", [128, 128])
    wrT_in = dram_in("l1_wrT", [1024, 1024])
    wkT_in = dram_in("l1_wkT", [1024, 1024])
    wvT_in = dram_in("l1_wvT", [1024, 1024])
    mur_in = dram_in("l1_mur", [128, DC])
    muk_in = dram_in("l1_muk", [128, DC])

    scr_ternc = nc.dram_tensor("scr_ternc", [1024, 4096], mybir.dt.bfloat16, kind="Internal").ap()
    scr_ternp = nc.dram_tensor("scr_ternp", [4096, 1024], mybir.dt.bfloat16, kind="Internal").ap()
    scr_scc = scratch("scr_scc", [128, 32])
    scr_scp = scratch("scr_scp", [128, 8])
    scr_ueff = scratch("scr_ueff", [16, 4096])
    scr_ipW = {pre: scratch(f"scr_ipW_{pre}", [512, 2048]) for pre in ("fwd", "bwd")}
    scr_ipcv = {pre: scratch(f"scr_ipcv_{pre}", [128, 16]) for pre in ("fwd", "bwd")}
    scr_ipcv_row = {pre: scratch(f"scr_ipcvr_{pre}", [1, 2048]) for pre in ("fwd", "bwd")}
    scr_outb = scratch("scr_outb", [512, T])
    scr_x1 = scratch("scr_x1", [D, T])
    scr_xn1 = scratch("scr_xn1", [D, T])
    scr_xcd = scratch("scr_xcd", [D, T])
    scr_kj = scratch("scr_kj", [D, 16])
    scr_yst = scratch("scr_yst", [D, T])
    scr_Xks = scratch("scr_Xks", [128, 128 * 8 * 128])
    scr_U2 = scratch("scr_U2", [128, 128 * 128])
    scr_Pm2 = scratch("scr_Pm2", [128, 128 * 128])

    out_t = nc.dram_tensor("out_t", [D, T], F32, kind="ExternalOutput").ap()
    dbg_xc2 = dbg_carr = None
    if upto in ("p1", "p1x"):
        scr_outb = nc.dram_tensor("dbg_outb", [512, T], F32, kind="ExternalOutput").ap()
    if upto == "p1x":
        dbg_xc2 = nc.dram_tensor("dbg_xc2", [D, T], F32, kind="ExternalOutput").ap()
        dbg_carr = nc.dram_tensor("dbg_carr", [128, NTILES * 128], F32, kind="ExternalOutput").ap()
    if upto == "p2":
        scr_x1 = nc.dram_tensor("dbg_x1", [D, T], F32, kind="ExternalOutput").ap()
        scr_xn1 = nc.dram_tensor("dbg_xn1", [D, T], F32, kind="ExternalOutput").ap()

    ones1 = sb("ones1", [1, 128])
    nones1 = sb("nones1", [1, 128])
    onesc = sb("onesc", [128, 1])
    epsc = sb("epsc", [1, 1])
    ident = sb("identt", [128, 128])
    p.emit("gpsimd", lambda g: g.memset(ones1[:], 1.0), writes=["ones1"])
    p.emit("gpsimd", lambda g: g.memset(nones1[:], -1.0), writes=["nones1"])
    p.emit("gpsimd", lambda g: g.memset(onesc[:], 1.0), writes=["onesc"])
    p.emit("gpsimd", lambda g: g.memset(epsc[:], EPS), writes=["epsc"])
    p.dma("sync", ident[:], ident_in, writes=["identt"])

    push_scope()
    wrow = sb("wrow", [128, 4096])
    wtrn = sb("wtrn", [128, 4096])
    tcol = sb("tcol", [128, 1])
    tcol2 = sb("tcol2", [128, 1])
    wTb = sb("wTb", [128, 128], mybir.dt.bfloat16)
    sccol_c = sb("sccol_c", [128, 32])
    sccol_p = sb("sccol_p", [128, 8])
    gcolB = sb("gcolB", [128, 1])
    ueffT = sb("ueffT", [16, 4096])
    pt0 = psum("pt0", [128, 128])
    ptcv = psum("ptcv", [1, 2048])

    def ternarize(src_ap, nrows_chunks, ncols, dst, sccol, ksc):
        for rc in range(nrows_chunks):
            p.dma("sync", wrow[:, 0:ncols], src_ap[rc * 128:(rc + 1) * 128, :],
                  writes=["wrow"])
            p.emit("vector", lambda v, s=ncols: v.reduce_sum(
                tcol[:], wrow[:, 0:s], axis=AX.X, apply_absolute_value=True),
                reads=["wrow"], writes=["tcol"])
            p.emit("vector", lambda v, s=ncols: v.tensor_scalar(
                tcol[:], tcol[:], 1.0 / s, 1e-5, ALU.mult, ALU.max),
                reads=["tcol"], writes=["tcol"])
            p.emit("vector", lambda v, rc=rc, sccol=sccol: v.tensor_copy(
                sccol[:, rc:rc + 1], tcol[:]),
                reads=["tcol"], writes=[ksc])
            p.emit("vector", lambda v: v.reciprocal(tcol2[:], tcol[:]),
                   reads=["tcol"], writes=["tcol2"])
            p.emit("vector", lambda v, s=ncols: v.tensor_scalar(
                wtrn[:, 0:s], wrow[:, 0:s], tcol2[:, 0:1], None, ALU.mult),
                reads=["wrow", "tcol2"], writes=["wtrn"])
            p.emit("scalar", lambda s_, s=ncols: s_.activation(
                wrow[:, 0:s], wtrn[:, 0:s], AF.Square),
                reads=["wtrn"], writes=["wrow"])
            p.emit("vector", lambda v, s=ncols: v.tensor_single_scalar(
                wrow[:, 0:s], wrow[:, 0:s], 0.25, ALU.is_ge),
                reads=["wrow"], writes=["wrow"])
            p.emit("scalar", lambda s_, s=ncols: s_.activation(
                wtrn[:, 0:s], wtrn[:, 0:s], AF.Sign),
                reads=["wtrn"], writes=["wtrn"])
            p.emit("vector", lambda v, s=ncols: v.tensor_mul(
                wtrn[:, 0:s], wtrn[:, 0:s], wrow[:, 0:s]),
                reads=["wtrn", "wrow"], writes=["wtrn"])
            for cc in range(ncols // 128):
                p.emit("tensor", lambda t, c=cc: t.transpose(
                    pt0[:], wtrn[:, c * 128:(c + 1) * 128], ident[:]),
                    reads=["wtrn", "identt"], writes=["pt0"])
                p.emit("scalar", lambda s_: s_.copy(wTb[:], pt0[:]),
                       reads=["pt0"], writes=["wTb"])
                p.dma("sync", dst[cc * 128:(cc + 1) * 128, rc * 128:(rc + 1) * 128],
                      wTb[:], reads=["wTb"])

    ternarize(wfc_in, 32, 1024, scr_ternc, sccol_c, "sccol_c")
    ternarize(wpj_in, 8, 4096, scr_ternp, sccol_p, "sccol_p")
    p.dma("sync", scr_scc, sccol_c[:], reads=["sccol_c"], writes=["scr_scc"])
    p.dma("sync", scr_scp, sccol_p[:], reads=["sccol_p"], writes=["scr_scp"])
    p.dma("sync", tcol[0:1, 0:1], gate_in, writes=["tcol"])
    p.emit("tensor", lambda t: t.matmul(pt0[:, 0:1], ones1[:], tcol[0:1, 0:1],
                                        start=True, stop=True),
           reads=["tcol", "ones1"], writes=["pt0"])
    p.emit("scalar", lambda s: s.copy(gcolB[:], pt0[:, 0:1]),
           reads=["pt0"], writes=["gcolB"])
    for rc in range(32):
        p.dma("sync", wrow[:, 0:16], ffuN_in[rc * 128:(rc + 1) * 128, :],
              writes=["wrow"])
        p.emit("vector", lambda v, rc=rc: v.reciprocal(
            tcol2[:], sccol_c[:, rc:rc + 1]),
            reads=["sccol_c"], writes=["tcol2"])
        p.emit("vector", lambda v: v.tensor_mul(tcol2[:], tcol2[:], gcolB[:]),
               reads=["tcol2", "gcolB"], writes=["tcol2"])
        p.emit("vector", lambda v: v.tensor_scalar(
            wrow[:, 0:16], wrow[:, 0:16], tcol2[:, 0:1], None, ALU.mult),
            reads=["wrow", "tcol2"], writes=["wrow"])
        p.emit("tensor", lambda t: t.transpose(
            pt0[0:16, 0:128], wrow[:, 0:16], ident[:]),
            reads=["wrow", "identt"], writes=["pt0"])
        p.emit("scalar", lambda s, rc=rc: s.copy(
            ueffT[:, rc * 128:(rc + 1) * 128], pt0[0:16, 0:128]),
            reads=["pt0"], writes=["ueffT"])
    p.dma("sync", scr_ueff, ueffT[:], reads=["ueffT"], writes=["scr_ueff"])
    nwc0 = sb("nwc0", [128, DC])
    nbc0 = sb("nbc0", [128, DC])
    p.dma("sync", nwc0[:], nw_in, writes=["nwc0"])
    p.dma("sync", nbc0[:], nb_in, writes=["nbc0"])
    for pre in ("fwd", "bwd"):
        kcs0 = [0, 1, 2, 3] if pre == "fwd" else [4, 5, 6, 7]
        for i, kc in enumerate(kcs0):
            p.dma("sync", wrow[:, 0:2048], mamba_w[pre]["ipT"][i * 128:(i + 1) * 128, :],
                  writes=["wrow"])
            p.emit("vector", lambda v, kc=kc: v.tensor_scalar(
                wtrn[:, 0:2048], wrow[:, 0:2048], nwc0[:, kc:kc + 1], None, ALU.mult),
                reads=["wrow", "nwc0"], writes=["wtrn"])
            p.dma("sync", scr_ipW[pre][i * 128:(i + 1) * 128, :], wtrn[:, 0:2048],
                  reads=["wtrn"], writes=[f"scr_ipW_{pre}"])
        for i, kc in enumerate(kcs0):
            p.dma("sync", wrow[:, 0:2048], mamba_w[pre]["ipT"][i * 128:(i + 1) * 128, :],
                  writes=["wrow"])
            for ns in range(4):
                p.emit("tensor", lambda t, kc=kc, ns=ns, i=i: t.matmul(
                    ptcv[0:1, ns * 512:(ns + 1) * 512], nbc0[:, kc:kc + 1],
                    wrow[:, ns * 512:(ns + 1) * 512], start=(i == 0), stop=(i == 3)),
                    reads=["wrow", "nbc0"], writes=["ptcv"])
        p.emit("scalar", lambda s, pre=pre: s.copy(wtrn[0:1, 0:2048], ptcv[0:1, 0:2048]),
               reads=["ptcv"], writes=["wtrn"])
        p.dma("sync", scr_ipcv_row[pre], wtrn[0:1, 0:2048], reads=["wtrn"],
              writes=[f"scr_ipcvr_{pre}"])
        p.emit("sync", lambda e, pre=pre: e.dma_start(
            out=wrow[:, 0:16].rearrange("p (c o) -> p c o", o=1),
            in_=scr_ipcv_row[pre].rearrange("one (c p) -> p c one", p=128),
            allow_slow_non_contiguous=True),
            reads=[f"scr_ipcvr_{pre}"], writes=["wrow"], is_dma=True)
        p.dma("sync", scr_ipcv[pre], wrow[:, 0:16], reads=["wrow"],
              writes=[f"scr_ipcv_{pre}"])
    p.barrier()
    pop_scope()

    push_scope()
    xt = sb("xt", [128, DC * TT])
    xn = sb("xn", [128, DC * TT])
    xcr = sb("xcr", [128, DC * (TT + 6)])
    xc2 = sb("xc2", [128, DC * TT])
    sz = sb("sz", [128, DC * TT])
    dtb = sb("dtb", [128, DC * TT])
    dtu = sb("dtu", [128, DC * TT])
    ya = sb("ya", [128, DC * TT])
    arena = sb("arena", [128, 4096])
    xdbl = sb("xdbl", [64, TT])
    carr = sb("carr", [128, 128])
    halo = sb("halo", [128, DC * 3])
    mrow_t = sb("mrow_t", [1, TT])
    vrow_t = sb("vrow_t", [1, TT])
    rstd_t = sb("rstd_t", [1, TT])
    nmr_t = sb("nmr_t", [1, TT])
    mrow, vrow = mrow_t[:], vrow_t[:]
    rstd, nmr = rstd_t[:], nmr_t[:]
    wopS = sb("wopS", [128, 2 * 8 * 128])
    wipS = wopS
    wxp = sb("wxp", [128, DC * 64])
    wdt = sb("wdt", [32, 1024])
    acols = sb("acols", [128, DC * 16])
    cwc = sb("cwc", [128, DC * 4])
    cbc = sb("cbc", [128, DC])
    dtbc = sb("dtbc", [128, DC])
    dvc = sb("dvc", [128, DC])
    wfs = sb("wfs", [128, 2 * 8 * 128], mybir.dt.bfloat16)
    wps = sb("wps", [128, 2 * 8 * 128], mybir.dt.bfloat16)
    xnh = sb("xnh", [128, DC * 256], mybir.dt.bfloat16)
    xnl = sb("xnl", [128, DC * 256], mybir.dt.bfloat16)
    scanB0 = sb("scanB0", [128, 2048])
    scanB1 = sb("scanB1", [128, 2048])
    r2h = scanB0[:].bitcast(mybir.dt.bfloat16)
    r2l = scanB1[:].bitcast(mybir.dt.bfloat16)
    r2tmp = sb("r2tmp", [128, 256])
    sccS = sb("sccS", [128, 32])
    scpS = sb("scpS", [128, 8])
    uTs = sb("uTs", [16, 128])
    vT = sb("vT", [128, DC * 16])
    fbT = sb("fbT", [16, 16])
    lrt = sb("lrt", [16, 512])
    lr1, lr2 = lrt[:, 0:256], lrt[:, 256:512]
    gcol = sb("gcol", [16, 1])

    ps_a = psum("ps_a", [128, 512])
    ps_b = psum("ps_b", [128, 512])
    ps_c = psum("ps_c", [128, 512])
    ps_d = psum("ps_d", [128, 512])
    ps_e = psum("ps_e", [128, 512])
    ps_f = psum("ps_f", [128, 512])
    ps_s1 = psum("ps_s1", [1, 512])
    ps_s2 = psum("ps_s2", [1, 512])


    ipcvS = sb("ipcvS", [128, 16])

    def load_dir_weights(pre):
        w = mamba_w[pre]
        p.dma("sync", ipcvS[:], scr_ipcv[pre], reads=[f"scr_ipcv_{pre}"],
              writes=["ipcvS"])
        p.dma("sync", wxp[:].rearrange("p (c o) -> p c o", o=64),
              w["xpT"].rearrange("(c p) o -> p c o", p=128), writes=["wxp"])
        p.dma("sync", wdt[:], w["dtT"], writes=["wdt"])
        p.dma("sync", acols[:], w["alog"], writes=["acols"])
        p.dma("sync", cwc[:], w["cw"], writes=["cwc"])
        p.dma("sync", cbc[:], w["cb"], writes=["cbc"])
        p.dma("sync", dtbc[:], w["dtb"], writes=["dtbc"])
        p.dma("sync", dvc[:], w["Dv"], writes=["dvc"])
        p.emit("scalar", lambda s: s.activation(acols[:], acols[:], AF.Exp),
               reads=["acols"], writes=["acols"])
        p.emit("vector", lambda v: v.tensor_scalar(acols[:], acols[:], -1.0, None, ALU.mult),
               reads=["acols"], writes=["acols"])

    def ln_stats(src, src_key):
        sqv = wfs[:].bitcast(F32)
        sq = sqv[:, 0:TT]
        for c in range(DC):
            p.emit("scalar", lambda s, c=c, sq=sq: s.activation(
                sq, src[:, c * TT:(c + 1) * TT], AF.Square),
                reads=[src_key], writes=["wfs0", "wfs1"])
            p.emit("tensor", lambda t, c=c: t.matmul(
                ps_s1[:], onesc[:], src[:, c * TT:(c + 1) * TT],
                start=(c == 0), stop=(c == DC - 1)),
                reads=[src_key, "onesc"], writes=["ps_s1"])
            p.emit("tensor", lambda t, c=c, sq=sq: t.matmul(
                ps_s2[:], onesc[:], sq, start=(c == 0), stop=(c == DC - 1)),
                reads=["wfs0", "wfs1", "onesc"], writes=["ps_s2"])
        p.emit("scalar", lambda s: s.activation(mrow, ps_s1[:], AF.Identity,
                                                scale=1.0 / D),
               reads=["ps_s1"], writes=["rows"])
        p.emit("scalar", lambda s: s.activation(vrow, ps_s2[:], AF.Identity,
                                                scale=1.0 / D),
               reads=["ps_s2"], writes=["rows"])
        srow2 = wps[:].bitcast(F32)[0:1, 512:1024]
        p.emit("scalar", lambda s, srow2=srow2: s.activation(srow2, mrow, AF.Square),
               reads=["rows"], writes=["wps0", "wps1"])
        p.emit("gpsimd", lambda g, srow2=srow2: g.tensor_sub(vrow, vrow, srow2),
               reads=["rows", "wps0", "wps1"], writes=["rows"])
        p.emit("scalar", lambda s: s.activation(rstd, vrow, AF.Ln,
                                                bias=epsc[0:1, 0:1]),
               reads=["rows", "epsc"], writes=["rstd_t"])
        p.emit("scalar", lambda s: s.activation(rstd, rstd, AF.Exp, scale=-0.5),
               reads=["rstd_t"], writes=["rstd_t"])
        p.emit("gpsimd", lambda g: g.tensor_mul(nmr, mrow, rstd),
               reads=["rows", "rstd_t"], writes=["nmr_t"])
        p.emit("tensor", lambda t: t.matmul(ps_c[:], ones1[:], rstd, start=True, stop=True),
               reads=["rstd_t", "ones1"], writes=["ps_c"])
        p.emit("tensor", lambda t: t.matmul(ps_d[:], nones1[:], nmr, start=True, stop=True),
               reads=["nmr_t", "nones1"], writes=["ps_d"])
        p.emit("scalar", lambda s, sqv=sqv: s.copy(sqv[:, 512:1024], ps_c[:]),
               reads=["ps_c"], writes=["wfs0", "wfs1"])
        p.emit("scalar", lambda s: s.copy(wps[:].bitcast(F32)[:, 0:512], ps_d[:]),
               reads=["ps_d"], writes=["wps0", "wps1"])

    def apply_ln(dst, dst_key, src, src_key):
        rb = wfs[:].bitcast(F32)[:, 512:1024]
        nb = wps[:].bitcast(F32)[:, 0:512]
        for c in range(DC):
            p.emit("vector", lambda v, c=c, rb=rb: v.tensor_mul(
                dst[:, c * TT:(c + 1) * TT], src[:, c * TT:(c + 1) * TT], rb),
                reads=[src_key, "wfs0", "wfs1"], writes=[dst_key])
            p.emit("vector", lambda v, c=c, nb=nb: v.tensor_add(
                dst[:, c * TT:(c + 1) * TT], dst[:, c * TT:(c + 1) * TT], nb),
                reads=[dst_key, "wps0", "wps1"], writes=[dst_key])

    def mamba_tile(pre, rev, companions=None):
        kcs = [4, 5, 6, 7] if rev else [0, 1, 2, 3]
        w = mamba_w[pre]
        for m in range(16):
            bank, bkey = (ps_a, "ps_a") if (m % 2 == 0) else (ps_b, "ps_b")
            buf = m % 2
            p.dma("sync", wipS[:, buf * 1024:buf * 1024 + 512].rearrange("p (c o) -> p c o", o=128),
                  scr_ipW[pre].rearrange("(c p) o -> p c o", p=128)[:, :, m * 128:(m + 1) * 128],
                  reads=[f"scr_ipW_{pre}"], writes=[f"wopS{buf}"])
            for i, kc in enumerate(kcs):
                p.emit("tensor", lambda t, kc=kc, i=i, buf=buf, bank=bank: t.matmul(
                    bank[:], wipS[:, buf * 1024 + i * 128:buf * 1024 + (i + 1) * 128],
                    xn[:, kc * TT:(kc + 1) * TT], start=(i == 0), stop=(i == 3)),
                    reads=[f"wopS{buf}", "xn"], writes=[bkey])
            if m < 8:
                p.emit("scalar", lambda s, m=m, bank=bank: s.activation(
                    xcr[:, m * (TT + 6) + 3:m * (TT + 6) + 3 + TT], bank[:],
                    AF.Identity, bias=ipcvS[:, m:m + 1]),
                    reads=[bkey, "ipcvS"], writes=["xcr", "xcrS"])
            else:
                p.emit("scalar", lambda s, m=m, bank=bank: s.activation(
                    sz[:, (m - 8) * TT:(m - 7) * TT], bank[:], AF.Silu,
                    bias=ipcvS[:, m:m + 1]),
                    reads=[bkey, "ipcvS"], writes=["sz"])
        for c in range(DC):
            base = c * (TT + 6)
            if rev:
                p.emit("vector", lambda v, base=base, c=c: v.tensor_copy(
                    xcr[:, base + 3 + TT:base + 6 + TT], halo[:, c * 3:(c + 1) * 3]),
                    reads=["halo"], writes=["xcr"])
            else:
                p.emit("vector", lambda v, base=base, c=c: v.tensor_copy(
                    xcr[:, base:base + 3], halo[:, c * 3:(c + 1) * 3]),
                    reads=["halo"], writes=["xcr"])
        for c in range(DC):
            base = c * (TT + 6)
            if rev:
                p.emit("vector", lambda v, base=base, c=c: v.tensor_copy(
                    halo[:, c * 3:(c + 1) * 3], xcr[:, base + 3:base + 6]),
                    reads=["xcr"], writes=["halo"])
            else:
                p.emit("vector", lambda v, base=base, c=c: v.tensor_copy(
                    halo[:, c * 3:(c + 1) * 3], xcr[:, base + TT:base + TT + 3]),
                    reads=["xcr"], writes=["halo"])
        for c in range(DC):
            base = c * (TT + 6) + 3
            offs = [(3, 0), (2, 1), (1, 2), (0, 3)] if rev else [(-3, 0), (-2, 1), (-1, 2), (0, 3)]
            o0, j0 = offs[0]
            p.emit("vector", lambda v, c=c, base=base, o0=o0, j0=j0: v.tensor_scalar(
                xc2[:, c * TT:(c + 1) * TT], xcr[:, base + o0:base + o0 + TT],
                cwc[:, c * 4 + j0:c * 4 + j0 + 1], None, ALU.mult),
                reads=["xcr", "cwc"], writes=["xc2"])
            for (o, j) in offs[1:]:
                p.emit("vector", lambda v, c=c, base=base, o=o, j=j: v.scalar_tensor_tensor(
                    xc2[:, c * TT:(c + 1) * TT], xcr[:, base + o:base + o + TT],
                    cwc[:, c * 4 + j:c * 4 + j + 1], xc2[:, c * TT:(c + 1) * TT],
                    ALU.mult, ALU.add),
                    reads=["xcr", "cwc", "xc2"], writes=["xc2"])
            p.emit("scalar", lambda s, c=c: s.activation(
                xc2[:, c * TT:(c + 1) * TT], xc2[:, c * TT:(c + 1) * TT], AF.Silu,
                bias=cbc[:, c:c + 1]),
                reads=["xc2", "cbc"], writes=["xc2"])
        for c in range(DC):
            p.emit("tensor", lambda t, c=c: t.matmul(
                ps_e[0:64, :], wxp[:, c * 64:(c + 1) * 64], xc2[:, c * TT:(c + 1) * TT],
                start=(c == 0), stop=(c == DC - 1)),
                reads=["wxp", "xc2"], writes=["ps_e"])
        p.emit("scalar", lambda s: s.copy(xdbl[:], ps_e[0:64, :]),
               reads=["ps_e"], writes=["xdbl"])
        for c in range(DC):
            bank, bkey = (ps_a, "ps_a") if (c % 2 == 0) else (ps_b, "ps_b")
            p.emit("tensor", lambda t, c=c, bank=bank: t.matmul(
                bank[:], wdt[:, c * 128:(c + 1) * 128], xdbl[0:32, :],
                start=True, stop=True),
                reads=["wdt", "xdbl"], writes=[bkey])
            p.emit("scalar", lambda s, c=c, bank=bank: s.activation(
                dtb[:, c * TT:(c + 1) * TT], bank[:], AF.Exp, bias=dtbc[:, c:c + 1]),
                reads=[bkey, "dtbc"], writes=["dtb"])
            p.emit("scalar", lambda s, c=c: s.activation(
                dtb[:, c * TT:(c + 1) * TT], dtb[:, c * TT:(c + 1) * TT], AF.Ln, bias=1.0),
                reads=["dtb"], writes=["dtb"])
        for c in range(DC):
            p.emit("vector", lambda v, c=c: v.tensor_mul(
                dtu[:, c * TT:(c + 1) * TT], dtb[:, c * TT:(c + 1) * TT],
                xc2[:, c * TT:(c + 1) * TT]),
                reads=["dtb", "xc2"], writes=["dtu"])
        dA_sets = [arena[:, 0:2048], xcr[:, 2048:4096]]
        dB_sets = [arena[:, 2048:4096], arena[:, 2048:4096]]
        kA_sets = [["arena"], ["xcrS"]]
        kB_sets = [["arena2"], ["arena2"]]
        comp_i = 0
        for n in range(16):
            selB = ident[0:64, 32 + n:33 + n].broadcast_to([64, 128])
            selC = ident[0:64, 48 + n:49 + n].broadcast_to([64, 128])
            p.emit("tensor", lambda t, selB=selB: t.matmul(
                ps_c[:], selB, xdbl[0:64, :], start=True, stop=True),
                reads=["xdbl", "identt"], writes=["ps_c"])
            p.emit("tensor", lambda t, selC=selC: t.matmul(
                ps_d[:], selC, xdbl[0:64, :], start=True, stop=True),
                reads=["xdbl", "identt"], writes=["ps_d"])
            bco = (n % 2) * TT
            p.emit("scalar", lambda s, bco=bco: s.copy(
                xcr[:, bco:bco + TT], ps_c[:]),
                reads=["ps_c"], writes=["xcr"])
            p.emit("scalar", lambda s, bco=bco: s.copy(
                xcr[:, 1024 + bco:1024 + bco + TT], ps_d[:]),
                reads=["ps_d"], writes=["xcr"])
            for ch in range(2):
                q = ch % 2
                dA, dB = dA_sets[q], dB_sets[q]
                kA, kB = kA_sets[q], kB_sets[q]
                c0 = ch * 4
                for ci in range(4):
                    c = c0 + ci
                    p.emit("scalar", lambda s, dA=dA, c=c, ci=ci, n=n: s.activation(
                        dA[:, ci * TT:(ci + 1) * TT], dtb[:, c * TT:(c + 1) * TT],
                        AF.Exp, scale=acols[:, c * 16 + n:c * 16 + n + 1]),
                        reads=["dtb", "acols"], writes=kA)
                bbc = xcr[:, bco:bco + TT].unsqueeze(1).broadcast_to([128, 4, TT])
                p.emit("gpsimd", lambda gg, dB=dB, c0=c0, bbc=bbc: gg.tensor_tensor(
                    dB[:].rearrange("p (c t) -> p c t", t=TT),
                    dtu[:, c0 * TT:(c0 + 4) * TT].rearrange("p (c t) -> p c t", t=TT),
                    bbc, ALU.mult),
                    reads=["dtu", "xcr"], writes=kB)
                for ci in range(4):
                    c = c0 + ci
                    col = c * 16 + n
                    if rev:
                        d0 = dA[:, ci * TT:(ci + 1) * TT][:, ::-1]
                        d1 = dB[:, ci * TT:(ci + 1) * TT][:, ::-1]
                    else:
                        d0 = dA[:, ci * TT:(ci + 1) * TT]
                        d1 = dB[:, ci * TT:(ci + 1) * TT]
                    p.emit("vector", lambda v, d0=d0, d1=d1, col=col: v.tensor_tensor_scan(
                        d0, d0, d1, carr[:, col:col + 1], ALU.mult, ALU.add),
                        reads=kA + kB + ["carr"], writes=kA)
                off = 0 if rev else TT - 1
                p.emit("scalar", lambda s, dA=dA, off=off, c0=c0, n=n: s.copy(
                    carr[:].rearrange("p (c n) -> p c n", n=16)[:, c0:c0 + 4, n:n + 1],
                    dA[:].rearrange("p (c t) -> p c t", t=TT)[:, :, off:off + 1]),
                    reads=kA, writes=["carr"])
                cbc_ap = xcr[:, 1024 + bco:1024 + bco + TT].unsqueeze(1).broadcast_to(
                    [128, 4, TT])
                p.emit("vector", lambda v, dA=dA, cbc_ap=cbc_ap: v.tensor_tensor(
                    dA[:].rearrange("p (c t) -> p c t", t=TT),
                    dA[:].rearrange("p (c t) -> p c t", t=TT),
                    cbc_ap, ALU.mult),
                    reads=kA + ["xcr"], writes=kA)
                ys, kys = dA, kA
                if n == 0:
                    p.emit("vector", lambda v, ys=ys, c0=c0: v.tensor_copy(
                        ya[:, c0 * TT:(c0 + 4) * TT], ys[:]),
                        reads=kys, writes=["ya"])
                else:
                    p.emit("vector", lambda v, ys=ys, c0=c0: v.tensor_add(
                        ya[:, c0 * TT:(c0 + 4) * TT], ya[:, c0 * TT:(c0 + 4) * TT],
                        ys[:]),
                        reads=["ya"] + kys, writes=["ya"])
            if companions:
                tgt = (n + 1) * len(companions) // 16
                while comp_i < tgt:
                    companions[comp_i]()
                    comp_i += 1
        if companions:
            while comp_i < len(companions):
                companions[comp_i]()
                comp_i += 1
        for c in range(DC):
            p.emit("vector", lambda v, c=c: v.scalar_tensor_tensor(
                dtb[:, c * TT:(c + 1) * TT], xc2[:, c * TT:(c + 1) * TT],
                dvc[:, c:c + 1], ya[:, c * TT:(c + 1) * TT], ALU.mult, ALU.add),
                reads=["xc2", "dvc", "ya"], writes=["dtb"])
            p.emit("vector", lambda v, c=c: v.tensor_mul(
                dtb[:, c * TT:(c + 1) * TT], dtb[:, c * TT:(c + 1) * TT],
                sz[:, c * TT:(c + 1) * TT]),
                reads=["dtb", "sz"], writes=["dtb"])
        dst, dstk = sz, "sz"
        moff = 0
        for m in range(4):
            bank, bkey = (ps_a, "ps_a") if (m % 2 == 0) else (ps_b, "ps_b")
            buf = m % 2
            p.dma("sync", wopS[:, buf * 1024:(buf + 1) * 1024].rearrange("p (c o) -> p c o", o=128),
                  w["opT"].rearrange("(c p) o -> p c o", p=128)[:, :, m * 128:(m + 1) * 128],
                  writes=[f"wopS{buf}"])
            for c in range(DC):
                p.emit("tensor", lambda t, m=m, c=c, buf=buf, bank=bank: t.matmul(
                    bank[:], wopS[:, buf * 1024 + c * 128:buf * 1024 + (c + 1) * 128],
                    dtb[:, c * TT:(c + 1) * TT], start=(c == 0), stop=(c == DC - 1)),
                    reads=[f"wopS{buf}", "dtb"], writes=[bkey])
            p.emit("scalar", lambda s, m=m, bank=bank: s.copy(
                dst[:, (moff + m) * TT:(moff + m + 1) * TT], bank[:]),
                reads=[bkey], writes=[dstk])

    def load_x_tile(ti):
        t0 = ti * TT
        src = xin.rearrange("(c p) t -> p c t", p=128)[:, :, t0:t0 + TT]
        p.dma("sync", xt[:].rearrange("p (c t) -> p c t", t=TT), src, writes=["xt"])

    xnlf = xnl[:].bitcast(F32)

    def pre_steps(g, ri):
        preT = scanB0 if ri == 0 else scanB1
        K = "scanB0" if ri == 0 else "scanB1"
        psb = ps_f if ri == 0 else ps_e
        KP = "ps_f" if ri == 0 else "ps_e"
        pb = 0

        def blk(i):
            return preT[:, i * 128:(i + 1) * 128]

        def xsl(r):
            return 2 * r if r <= 3 else 8 + 2 * (r % 2)

        Upre, PTpre, Pmpre = blk(12), blk(13), blk(14)
        VCp = preT[:, 15 * 128:15 * 128 + 16]
        kev = preT[0:8, 15 * 128 + 16:15 * 128 + 32]

        def ucl(buf, j0, j1):
            return buf.rearrange("p (d s) -> p d s", d=8)[:, :, j0:j1]

        lvl = globals().get("_PRE_LVL", 9)
        if lvl < 1:
            return
        p.dma("sync", blk(0), xblk_in[:, g * 128:(g + 1) * 128], writes=[K])
        p.dma("sync", blk(1), zblk_in[:, g * 128:(g + 1) * 128], writes=[K])
        p.dma("sync", PTpre, bce_in[:, g * 128:(g + 1) * 128], writes=[K])
        p.dma("sync", Upre, cce_in[:, g * 128:(g + 1) * 128], writes=[K])
        yield
        if lvl < 2:
            return
        for r in range(11):
            xs, xd = xsl(r), xsl(r + 1)
            p.emit("tensor", lambda t, xs=xs: t.matmul(
                psb[:, 0:128], blk(xs + 1), blk(xs),
                start=True, stop=True),
                reads=[K], writes=[KP])
            p.emit("tensor", lambda t, xs=xs: t.matmul(
                psb[:, 128:256], blk(xs), blk(xs + 1),
                start=True, stop=True),
                reads=[K], writes=[KP])
            yield
            if globals().get("_PRE_EVAC", 1):
                p.emit("scalar", lambda s_, xd=xd: s_.copy(
                    preT[:, xd * 128:(xd + 2) * 128], psb[:, 0:256]),
                    reads=[KP], writes=[K])
            if r + 1 >= 4 and lvl >= 3:
                p.dma("sync", scr_Xks[:, (g * 8 + r - 3) * 128:
                                        (g * 8 + r - 2) * 128],
                      blk(xd), reads=[K], writes=["scr_Xks"])
            yield
        if lvl < 3:
            return
        p.emit("tensor", lambda t: t.matmul(
            psb[:, 0:8], blk(1), ucl(Upre, 0, 1), start=True, stop=True),
            reads=[K], writes=[KP])
        yield
        p.emit("scalar", lambda s_: s_.copy(
            ucl(Upre, 0, 1), psb[:, 0:8].rearrange("p (d s) -> p d s", s=1)),
            reads=[KP], writes=[K])
        yield
        for r in range(4):
            nj = 2 ** r
            p.emit("tensor", lambda t, r=r, nj=nj: t.matmul(
                psb[:, 0:8 * nj], blk(2 * r + 1), ucl(Upre, 0, nj),
                start=True, stop=True),
                reads=[K], writes=[KP])
            yield
            p.emit("scalar", lambda s_, nj=nj: s_.copy(
                ucl(Upre, nj, 2 * nj),
                psb[:, 0:8 * nj].rearrange("p (d s) -> p d s", s=nj)),
                reads=[KP], writes=[K])
            yield
        if lvl < 4:
            return
        for r in range(4):
            nj = 2 ** r
            p.emit("tensor", lambda t, r=r, nj=nj: t.matmul(
                psb[:, 0:8 * nj], blk(2 * r), ucl(PTpre, 16 - nj, 16),
                start=True, stop=True),
                reads=[K], writes=[KP])
            yield
            p.emit("scalar", lambda s_, nj=nj: s_.copy(
                ucl(PTpre, 16 - 2 * nj, 16 - nj),
                psb[:, 0:8 * nj].rearrange("p (d s) -> p d s", s=nj)),
                reads=[KP], writes=[K])
            yield
        if lvl < 5:
            return
        p.emit("tensor", lambda t: t.transpose(
            psb[:, 0:128], PTpre, ident[:]),
            reads=[K, "identt"], writes=[KP])
        yield
        p.emit("scalar", lambda s_: s_.copy(Pmpre, psb[:, 0:128]),
               reads=[KP], writes=[K])
        p.dma("sync", scr_U2[:, g * 128:(g + 1) * 128], Upre,
              reads=[K], writes=["scr_U2"])
        p.dma("sync", scr_Pm2[:, g * 128:(g + 1) * 128], Pmpre,
              reads=[K], writes=["scr_Pm2"])
        yield
        if lvl < 6:
            return
        p.emit("vector", lambda v: v.reduce_sum(
            VCp, PTpre.rearrange("p (d s) -> p s d", d=8), axis=AX.X),
            reads=[K], writes=[K])
        p.emit("vector", lambda v, g=g: v.tensor_scalar(
            VCp, VCp, xnlf[:, g:g + 1], None, ALU.mult),
            reads=[K, "xnl"], writes=[K])
        yield
        p.emit("tensor", lambda t: t.matmul(
            psb[0:8, 128:144], xnlf[:, 128:136], VCp,
            start=True, stop=True),
            reads=[K, "xnl"], writes=[KP])
        yield
        p.emit("scalar", lambda s_, kev=kev: s_.copy(kev, psb[0:8, 128:144]),
               reads=[KP], writes=[K])
        p.dma("sync", scr_kj[g * 8:(g + 1) * 8, :], kev,
              reads=[K], writes=["scr_kj"])

    def pre_pair(gA, gB):
        its = [pre_steps(gA, 0), pre_steps(gB, 1)]
        done = [False, False]
        while not all(done):
            for i, it in enumerate(its):
                if not done[i]:
                    try:
                        next(it)
                    except StopIteration:
                        done[i] = True

    load_dir_weights("bwd")
    p.emit("gpsimd", lambda g: g.memset(carr[:], 0.0), writes=["carr"])
    p.emit("gpsimd", lambda g: g.memset(halo[:], 0.0), writes=["halo"])
    p.dma("sync", xnlf[:, 0:128], l1cc_in, writes=["xnl"])
    p.dma("sync", xnlf[:, 128:136], obd_in, writes=["xnl"])
    PRE_ON = globals().get("_PRE_ON", True)
    if PRE_ON:
        for pr in range(8):
            pre_pair(2 * pr, 2 * pr + 1)
    for ti in range(NTILES - 1, -1, -1):
        gb = 16 + (NTILES - 1 - ti) * 14
        comps = [(lambda gA=gA: pre_pair(gA, gA + 1))
                 for gA in range(gb, gb + 14, 2)] if PRE_ON else []
        load_x_tile(ti)
        ln_stats(xt, "xt")
        apply_ln(xn, "xn", xt, "xt")
        mamba_tile("bwd", rev=True, companions=comps)
        t0 = ti * TT
        p.dma("sync", scr_outb.rearrange("(c p) t -> p c t", p=128)[:, :, t0:t0 + TT],
              sz[:, 0:4 * TT].rearrange("p (c t) -> p c t", t=TT), reads=["sz"])
        if upto == "p1x":
            p.dma("sync", dbg_xc2.rearrange("(c p) t -> p c t", p=128)[:, :, t0:t0 + TT],
                  xc2[:].rearrange("p (c t) -> p c t", t=TT), reads=["xc2"])
            p.dma("sync", dbg_carr[:, ti * 128:(ti + 1) * 128], carr[:], reads=["carr"])
    if upto in ("p1", "p1x"):
        p.final_dma_wait()
        p.finalize()
        return nc

    load_dir_weights("fwd")
    p.emit("gpsimd", lambda g: g.memset(carr[:], 0.0), writes=["carr"])
    p.emit("gpsimd", lambda g: g.memset(halo[:], 0.0), writes=["halo"])
    p.dma("sync", vT[:].rearrange("p (c o) -> p c o", o=16),
          ffv_in.rearrange("(c p) o -> p c o", p=128), writes=["vT"])
    p.dma("sync", fbT[:], ffb_in, writes=["fbT"])
    p.dma("sync", sccS[:], scr_scc, reads=["scr_scc"], writes=["sccS"])
    p.dma("sync", scpS[:], scr_scp, reads=["scr_scp"], writes=["scpS"])

    def ffn_casts(s0):
        for c in range(DC):
            p.emit("gpsimd", lambda gg, c=c, s0=s0: gg.tensor_copy(
                xnh[:, c * 256:(c + 1) * 256], xn[:, c * TT + s0:c * TT + s0 + 256]),
                reads=["xn"], writes=["xnh"])
            p.emit("gpsimd", lambda gg, c=c, s0=s0: gg.tensor_sub(
                xnl[:, c * 256:(c + 1) * 256], xn[:, c * TT + s0:c * TT + s0 + 256],
                xnh[:, c * 256:(c + 1) * 256]),
                reads=["xn", "xnh"], writes=["xnl"])

    def ffn_lr(s0):
        for c in range(DC):
            p.emit("tensor", lambda t, c=c, s0=s0: t.matmul(
                ps_e[0:16, 0:256], vT[:, c * 16:(c + 1) * 16],
                xn[:, c * TT + s0:c * TT + s0 + 256],
                start=(c == 0), stop=(c == DC - 1)),
                reads=["vT", "xn"], writes=["ps_e"])
        p.emit("scalar", lambda s: s.copy(lr1, ps_e[0:16, 0:256]),
               reads=["ps_e"], writes=["lrt"])
        p.emit("tensor", lambda t: t.matmul(
            ps_e[0:16, 0:256], fbT[:], lr1, start=True, stop=True),
            reads=["fbT", "lrt"], writes=["ps_e"])
        p.emit("scalar", lambda s: s.copy(lr2, ps_e[0:16, 0:256]),
               reads=["ps_e"], writes=["lrt"])

    def ffn_chain(hc):
        hc16 = hc % 16
        bank, bkey = (ps_a, "ps_a") if (hc % 2 == 0) else (ps_b, "ps_b")
        buf = hc % 2
        p.dma("sync", wfs[:, buf * 1024:(buf + 1) * 1024].rearrange(
            "p (c o) -> p c o", o=128),
            scr_ternc.rearrange("(c p) o -> p c o", p=128)[
                :, :, hc * 128:(hc + 1) * 128],
            reads=["scr_ternc"], writes=[f"wfs{buf}"])
        p.dma("scalar", uTs[:], scr_ueff[:, hc * 128:(hc + 1) * 128],
              reads=["scr_ueff"], writes=["uTs"])
        for c in range(DC):
            p.emit("tensor", lambda t, c=c, buf=buf, bank=bank: t.matmul(
                bank[:, 0:256],
                wfs[:, buf * 1024 + c * 128:buf * 1024 + (c + 1) * 128],
                xnh[:, c * 256:(c + 1) * 256],
                start=(c == 0), stop=False),
                reads=[f"wfs{buf}", "xnh"], writes=[bkey])
            p.emit("tensor", lambda t, c=c, buf=buf, bank=bank: t.matmul(
                bank[:, 0:256],
                wfs[:, buf * 1024 + c * 128:buf * 1024 + (c + 1) * 128],
                xnl[:, c * 256:(c + 1) * 256],
                start=False, stop=False),
                reads=[f"wfs{buf}", "xnl"], writes=[bkey])
        p.emit("tensor", lambda t, bank=bank: t.matmul(
            bank[:, 0:256], uTs[:], lr2, start=False, stop=True),
            reads=["uTs", "lrt"], writes=[bkey])
        rt = r2tmp[:]
        rtk = "r2tmp"
        p.emit("scalar", lambda s, rt=rt, hc=hc, bank=bank: s.activation(
            rt, bank[:, 0:256], AF.Relu, scale=sccS[:, hc:hc + 1]),
            reads=[bkey, "sccS"], writes=[rtk])
        p.emit("scalar", lambda s, rt=rt: s.activation(rt, rt, AF.Square),
               reads=[rtk], writes=[rtk])
        p.emit("scalar", lambda s, rt=rt, hc16=hc16: s.copy(
            r2h[:, hc16 * 256:(hc16 + 1) * 256], rt),
            reads=[rtk], writes=["scanB0"])
        p.emit("gpsimd", lambda gg, rt=rt, hc16=hc16: gg.tensor_sub(
            r2l[:, hc16 * 256:(hc16 + 1) * 256], rt,
            r2h[:, hc16 * 256:(hc16 + 1) * 256]),
            reads=[rtk, "scanB0"], writes=["scanB1"])

    def ffn_wave(s0, half, wv):
        for bi, (bank, bkey) in enumerate([(ps_e, "ps_e"), (ps_f, "ps_f")]):
            dc_ = wv * 2 + bi
            for go in range(2):
                buf = (dc_ * 2 + go) % 2
                p.dma("sync", wps[:, buf * 1024:(buf + 1) * 1024].rearrange(
                    "p (c o) -> p c o", o=128),
                    scr_ternp.rearrange("(c p) o -> p c o", p=128)[
                        :, half * 16 + go * 8:half * 16 + go * 8 + 8,
                        dc_ * 128:(dc_ + 1) * 128],
                    reads=["scr_ternp"], writes=[f"wps{buf}"])
                for c8 in range(8):
                    hc16 = go * 8 + c8
                    p.emit("tensor", lambda t, c8=c8, hc16=hc16, buf=buf,
                           bank=bank: t.matmul(
                        bank[:, 0:256],
                        wps[:, buf * 1024 + c8 * 128:buf * 1024 + (c8 + 1) * 128],
                        r2h[:, hc16 * 256:(hc16 + 1) * 256],
                        start=(hc16 == 0), stop=False),
                        reads=[f"wps{buf}", "scanB0"], writes=[bkey])
                    p.emit("tensor", lambda t, c8=c8, hc16=hc16, buf=buf,
                           bank=bank: t.matmul(
                        bank[:, 0:256],
                        wps[:, buf * 1024 + c8 * 128:buf * 1024 + (c8 + 1) * 128],
                        r2l[:, hc16 * 256:(hc16 + 1) * 256],
                        start=False, stop=(hc16 == 15)),
                        reads=[f"wps{buf}", "scanB1"], writes=[bkey])
            p.emit("vector", lambda v, dc_=dc_, s0=s0, bank=bank:
                   v.scalar_tensor_tensor(
                xn[:, dc_ * TT + s0:dc_ * TT + s0 + 256], bank[:, 0:256],
                scpS[:, dc_:dc_ + 1], xn[:, dc_ * TT + s0:dc_ * TT + s0 + 256],
                ALU.mult, ALU.add),
                reads=[bkey, "scpS", "xn"], writes=["xn"])

    for ti in range(NTILES):
        t0 = ti * TT
        load_x_tile(ti)
        ln_stats(xt, "xt")
        apply_ln(xn, "xn", xt, "xt")
        ffn_casts(0)
        ffn_lr(0)
        comp = []
        for sub in range(2):
            s0 = sub * 256
            if sub == 1:
                comp.append(lambda: (ffn_casts(256), ffn_lr(256)))
            for half in range(2):
                for hc16 in range(16):
                    comp.append(lambda hc=half * 16 + hc16: ffn_chain(hc))
                for wv in range(4):
                    comp.append(lambda s0=s0, half=half, wv=wv:
                                ffn_wave(s0, half, wv))
        mamba_tile("fwd", rev=False, companions=comp)
        p.dma("sync", sz[:, 4 * TT:8 * TT].rearrange("p (c t) -> p c t", t=TT),
              scr_outb.rearrange("(c p) t -> p c t", p=128)[:, :, t0:t0 + TT],
              writes=["sz"])
        for c in range(DC):
            p.emit("gpsimd", lambda gg, c=c: gg.tensor_add(
                xn[:, c * TT:(c + 1) * TT], xn[:, c * TT:(c + 1) * TT],
                xt[:, c * TT:(c + 1) * TT]),
                reads=["xn", "xt"], writes=["xn"])
            p.emit("gpsimd", lambda gg, c=c: gg.tensor_add(
                xn[:, c * TT:(c + 1) * TT], xn[:, c * TT:(c + 1) * TT],
                sz[:, c * TT:(c + 1) * TT]),
                reads=["xn", "sz"], writes=["xn"])
        p.dma("sync", scr_x1.rearrange("(c p) t -> p c t", p=128)[:, :, t0:t0 + TT],
              xn[:].rearrange("p (c t) -> p c t", t=TT), reads=["xn"], writes=["scr_x1"])
        ln_stats(xn, "xn")
        apply_ln(xt, "xt", xn, "xn")
        p.dma("sync", scr_xn1.rearrange("(c p) t -> p c t", p=128)[:, :, t0:t0 + TT],
              xt[:].rearrange("p (c t) -> p c t", t=TT), reads=["xt"], writes=["scr_xn1"])
        for c in range(DC):
            p.emit("gpsimd", lambda gg, c=c: gg.tensor_copy(
                xc2[:, c * TT:(c + 1) * TT].rearrange("p (s c2) -> p s c2", s=LC),
                xt[:, c * TT:(c + 1) * TT].rearrange("p (c2 s) -> p s c2", s=LC)),
                reads=["xt"], writes=["xc2"])
        for c in range(DC):
            p.dma("sync",
                  scr_xcd.rearrange("(c p) (s c2) -> p c s c2", p=128, s=LC)[
                      :, c, :, (t0 // LC):(t0 // LC) + TT // LC],
                  xc2[:, c * TT:(c + 1) * TT].rearrange(
                      "p (s c2) -> p s c2", s=LC),
                  reads=["xc2"], writes=["scr_xcd"])

    if upto == "p2":
        p.final_dma_wait()
        p.finalize()
        return nc

    p.barrier()
    pop_scope()
    push_scope()

    NB = 4
    Xkb = [sb(f"Xk{i}", [128, 8 * 128]) for i in range(NB)]
    Ub = [sb(f"Ub{i}", [128, 128]) for i in range(NB)]
    Pb = [sb(f"Pb{i}", [128, 128]) for i in range(NB)]
    Hb = [sb(f"Hb{i}", [128, 256]) for i in range(NB)]
    wb = [sb(f"wb{i}", [128, 256]) for i in range(NB)]
    xgb = [sb(f"xgb{i}", [128, 256]) for i in range(NB)]
    ysb = [sb(f"ysb{i}", [128, 256]) for i in range(NB)]
    ps_gs = [psum(f"ps_gs{i}", [128, 256]) for i in range(NB)]
    ps_g1 = psum("ps_g1", [128, 256])
    ps_g2 = psum("ps_g2", [128, 256])

    for g_ in range(128):
        s_ = g_ % NB
        ps_g0 = ps_gs[s_]
        evac_eng = "scalar" if g_ % 2 == 0 else "vector"
        X, U, Pm, H, W, XG, YS = (Xkb[s_], Ub[s_], Pb[s_], Hb[s_], wb[s_],
                                  xgb[s_], ysb[s_])
        kX, kU, kP, kH, kW, kXG, kYS = (
            f"X{s_}", f"U{s_}", f"P{s_}", f"H{s_}", f"W{s_}", f"XG{s_}",
            f"YS{s_}")
        kPS = f"ps_gs{s_}"
        p.dma("gpsimd", X[:], scr_Xks[:, g_ * 1024:(g_ + 1) * 1024],
              reads=["scr_Xks"], writes=[kX])
        p.dma("sync", U[:], scr_U2[:, g_ * 128:(g_ + 1) * 128],
              reads=["scr_U2"], writes=[kU])
        p.dma("sync", Pm[:], scr_Pm2[:, g_ * 128:(g_ + 1) * 128],
              reads=["scr_Pm2"], writes=[kP])
        p.dma("sync", XG[:],
              scr_xcd[8 * g_:8 * g_ + 8, :].rearrange("dlt (s c) -> (dlt s) c", c=256),
              reads=["scr_xcd"], writes=[kXG])
        p.emit("tensor", lambda t, Pm=Pm, XG=XG: t.matmul(
            ps_g1[:], Pm[:], XG[:], start=True, stop=True),
            reads=[kP, kXG], writes=["ps_g1"])
        p.emit(evac_eng, lambda s, _ee=evac_eng, W=W: (s.copy if _ee == "scalar" else s.tensor_copy)(W[:], ps_g1[:]),
               reads=["ps_g1"], writes=[kW])
        p.emit("gpsimd", lambda gg, H=H: gg.memset(H[:, 0:1], 0.0), writes=[kH])
        p.emit("vector", lambda v, H=H, W=W: v.tensor_copy(H[:, 1:256], W[:, 0:255]),
               reads=[kW], writes=[kH])
        for j in range(8):
            sh = 2 ** j
            p.emit("tensor", lambda t, X=X, H=H, j=j, sh=sh: t.matmul(
                ps_g0[:, sh:256], X[:, j * 128:(j + 1) * 128],
                H[:, 0:256 - sh], start=True, stop=True),
                reads=[kX, kH], writes=[kPS])
            p.emit("vector", lambda v, H=H, sh=sh: v.tensor_add(
                H[:, sh:256], H[:, sh:256], ps_g0[:, sh:256]),
                reads=[kPS, kH], writes=[kH])
        p.emit("tensor", lambda t, U=U, H=H: t.matmul(
            ps_g2[:], U[:], H[:], start=True, stop=True),
            reads=[kU, kH], writes=["ps_g2"])
        p.emit(evac_eng, lambda s, _ee=evac_eng, YS=YS: (s.copy if _ee == "scalar" else s.tensor_copy)(YS[:], ps_g2[:]),
               reads=["ps_g2"], writes=[kYS])
        p.dma("sync",
              scr_yst[8 * g_:8 * g_ + 8, :].rearrange("dlt (s c) -> (dlt s) c", c=256),
              YS[:], reads=[kYS], writes=["scr_yst"])

    p.barrier()
    pop_scope()
    push_scope()

    x1t = sb("x1t", [128, DC * TT])
    xn1t = sb("xn1t", [128, DC * TT])
    yt = sb("yt", [128, DC * TT])
    rbuf = sb("rbuf", [128, DC * TT])
    kact = sb("kact", [128, DC * TT])
    tmpd = sb("tmpd", [128, TT])
    kcols = sb("kcols", [128, DC * 16])
    lnw1 = sb("lnw1", [128, DC])
    lnb1 = sb("lnb1", [128, DC])
    dmc = sb("dmc", [128, DC])
    boc = sb("boc", [128, DC])
    murc = sb("murc", [128, DC])
    mukc = sb("mukc", [128, DC])
    xprevh = sb("xprevh", [128, DC])
    wS = sb("wS", [128, 2 * 8 * 128])
    mrow2 = sb("mrow2", [1, TT])
    vrow2 = sb("vrow2", [1, TT])
    rstd2 = sb("rstd2", [1, TT])
    nmr2 = sb("nmr2", [1, TT])
    srow3 = sb("srow3", [1, TT])
    pe_a = psum("pe_a", [128, 512])
    pe_b = psum("pe_b", [128, 512])
    pe_c = psum("pe_c", [128, 512])
    pe_d = psum("pe_d", [128, 512])
    pe_s1 = psum("pe_s1", [1, 512])
    pe_s2 = psum("pe_s2", [1, 512])

    p.dma("sync", kcols[:].rearrange("p (c j) -> p c j", j=16),
          scr_kj.rearrange("(c p) j -> p c j", p=128), reads=["scr_kj"], writes=["kcols"])
    p.dma("sync", lnw1[:], l1lnw_in, writes=["lnw1"])
    p.dma("sync", lnb1[:], l1lnb_in, writes=["lnb1"])
    p.dma("sync", dmc[:], l1dm_in, writes=["dmc"])
    p.dma("sync", boc[:], bo_in, writes=["boc"])
    p.dma("sync", murc[:], mur_in, writes=["murc"])
    p.dma("sync", mukc[:], muk_in, writes=["mukc"])
    p.emit("gpsimd", lambda gg: gg.memset(xprevh[:], 0.0), writes=["xprevh"])

    def stats2(src, src_key):
        sq = tmpd[:]
        for c in range(DC):
            p.emit("scalar", lambda s, c=c: s.activation(
                sq, src[:, c * TT:(c + 1) * TT], AF.Square),
                reads=[src_key], writes=["tmpd"])
            p.emit("tensor", lambda t, c=c: t.matmul(
                pe_s1[:], onesc[:], src[:, c * TT:(c + 1) * TT],
                start=(c == 0), stop=(c == DC - 1)),
                reads=[src_key, "onesc"], writes=["pe_s1"])
            p.emit("tensor", lambda t, c=c: t.matmul(
                pe_s2[:], onesc[:], sq, start=(c == 0), stop=(c == DC - 1)),
                reads=["tmpd", "onesc"], writes=["pe_s2"])
        p.emit("vector", lambda v: v.tensor_scalar(
            mrow2[:], pe_s1[:], 1.0 / D, None, ALU.mult), reads=["pe_s1"], writes=["r2w"])
        p.emit("vector", lambda v: v.tensor_scalar(
            vrow2[:], pe_s2[:], 1.0 / D, EPS, ALU.mult, ALU.add),
            reads=["pe_s2"], writes=["r2w"])
        p.emit("scalar", lambda s: s.activation(srow3[:], mrow2[:], AF.Square),
               reads=["r2w"], writes=["r2s"])
        p.emit("vector", lambda v: v.tensor_sub(vrow2[:], vrow2[:], srow3[:]),
               reads=["r2w", "r2s"], writes=["r2w"])
        p.emit("scalar", lambda s: s.activation(rstd2[:], vrow2[:], AF.Ln),
               reads=["r2w"], writes=["r2r"])
        p.emit("scalar", lambda s: s.activation(rstd2[:], rstd2[:], AF.Exp, scale=-0.5),
               reads=["r2r"], writes=["r2r"])
        p.emit("vector", lambda v: v.scalar_tensor_tensor(
            nmr2[:], mrow2[:], -1.0, rstd2[:], ALU.mult, ALU.mult),
            reads=["r2w", "r2r"], writes=["r2n"])
        p.emit("tensor", lambda t: t.matmul(pe_c[:], ones1[:], rstd2[:],
                                            start=True, stop=True),
               reads=["r2r", "ones1"], writes=["pe_c"])
        p.emit("tensor", lambda t: t.matmul(pe_d[:], ones1[:], nmr2[:],
                                            start=True, stop=True),
               reads=["r2n", "ones1"], writes=["pe_d"])

    def stream_mm(win, dst_psum, dkey, mov, movkey, m):
        buf = m % 2
        p.dma("sync", wS[:, buf * 1024:(buf + 1) * 1024].rearrange(
            "p (c o) -> p c o", o=128),
            win.rearrange("(c p) o -> p c o", p=128)[:, :, m * 128:(m + 1) * 128],
            writes=[f"wS{buf}"])
        for c in range(DC):
            p.emit("tensor", lambda t, c=c, buf=buf, dst_psum=dst_psum, mov=mov: t.matmul(
                dst_psum[:], wS[:, buf * 1024 + c * 128:buf * 1024 + (c + 1) * 128],
                mov[:, c * TT:(c + 1) * TT], start=(c == 0), stop=(c == DC - 1)),
                reads=[f"wS{buf}", movkey], writes=[dkey])

    for ti in range(NTILES):
        t0 = ti * TT
        p.dma("sync", x1t[:].rearrange("p (c t) -> p c t", t=TT),
              scr_x1.rearrange("(c p) t -> p c t", p=128)[:, :, t0:t0 + TT],
              reads=["scr_x1"], writes=["x1t"])
        p.dma("sync", xn1t[:].rearrange("p (c t) -> p c t", t=TT),
              scr_xn1.rearrange("(c p) t -> p c t", p=128)[:, :, t0:t0 + TT],
              reads=["scr_xn1"], writes=["xn1t"])
        for c in range(DC):
            p.emit("vector", lambda v, c=c: v.tensor_scalar(
                yt[:, c * TT:(c + 1) * TT], xn1t[:, c * TT:(c + 1) * TT],
                dmc[:, c:c + 1], None, ALU.mult),
                reads=["xn1t", "dmc"], writes=["yt"])
            for j in range(16):
                nsub = LC - j
                src = xn1t[:, c * TT:(c + 1) * TT].rearrange(
                    "p (c2 s) -> p c2 s", s=LC)[:, :, 0:nsub]
                dst_ap = yt[:, c * TT:(c + 1) * TT].rearrange(
                    "p (c2 s) -> p c2 s", s=LC)[:, :, j:LC]
                kj = c * 16 + (15 - j)
                p.emit("vector", lambda v, src=src, dst_ap=dst_ap, kj=kj:
                       v.scalar_tensor_tensor(
                    dst_ap, src, kcols[:, kj:kj + 1], dst_ap,
                    ALU.mult, ALU.add),
                    reads=["xn1t", "kcols", "yt"], writes=["yt"])
            p.dma("sync", tmpd[:].rearrange("p (s c2) -> p s c2", c2=TT // LC),
                  scr_yst[c * 128:(c + 1) * 128, :].rearrange(
                      "p (s c2) -> p s c2", c2=256)[:, :, t0 // LC:t0 // LC + TT // LC],
                  reads=["scr_yst"], writes=["tmpd"])
            p.emit("vector", lambda v, c=c: v.tensor_add(
                yt[:, c * TT:(c + 1) * TT],
                yt[:, c * TT:(c + 1) * TT],
                tmpd[:].rearrange("p (s c2) -> p c2 s", c2=TT // LC)),
                reads=["yt", "tmpd"], writes=["yt"])
        stats2(yt, "yt")
        for c in range(DC):
            p.emit("vector", lambda v, c=c: v.tensor_mul(
                yt[:, c * TT:(c + 1) * TT], yt[:, c * TT:(c + 1) * TT], pe_c[:]),
                reads=["yt", "pe_c"], writes=["yt"])
            p.emit("vector", lambda v, c=c: v.tensor_add(
                yt[:, c * TT:(c + 1) * TT], yt[:, c * TT:(c + 1) * TT], pe_d[:]),
                reads=["yt", "pe_d"], writes=["yt"])
            p.emit("vector", lambda v, c=c: v.tensor_scalar(
                yt[:, c * TT:(c + 1) * TT], yt[:, c * TT:(c + 1) * TT],
                lnw1[:, c:c + 1], lnb1[:, c:c + 1], ALU.mult, ALU.add),
                reads=["yt", "lnw1", "lnb1"], writes=["yt"])
            p.emit("scalar", lambda s, c=c: s.activation(
                yt[:, c * TT:(c + 1) * TT], yt[:, c * TT:(c + 1) * TT], AF.Gelu),
                reads=["yt"], writes=["yt"])
        for m in range(DC):
            bank, bkey = (pe_a, "pe_a") if m % 2 == 0 else (pe_b, "pe_b")
            stream_mm(woT_in, bank, bkey, yt, "yt", m)
            p.emit("scalar", lambda s, m=m, bank=bank: s.activation(
                rbuf[:, m * TT:(m + 1) * TT], bank[:], AF.Identity,
                bias=boc[:, m:m + 1]),
                reads=[bkey, "boc"], writes=["rbuf"])
            p.emit("vector", lambda v, m=m: v.tensor_add(
                x1t[:, m * TT:(m + 1) * TT], x1t[:, m * TT:(m + 1) * TT],
                rbuf[:, m * TT:(m + 1) * TT]),
                reads=["x1t", "rbuf"], writes=["x1t"])
            p.emit("vector", lambda v, m=m: v.tensor_add(
                x1t[:, m * TT:(m + 1) * TT], x1t[:, m * TT:(m + 1) * TT],
                xn1t[:, m * TT:(m + 1) * TT]),
                reads=["x1t", "xn1t"], writes=["x1t"])
        for c in range(DC):
            p.emit("vector", lambda v, c=c: v.tensor_copy(
                tmpd[:, 1:TT], xn1t[:, c * TT:(c + 1) * TT - 1]),
                reads=["xn1t"], writes=["tmpd"])
            p.emit("vector", lambda v, c=c: v.tensor_copy(
                tmpd[:, 0:1], xprevh[:, c:c + 1]),
                reads=["xprevh"], writes=["tmpd"])
            p.emit("vector", lambda v, c=c: v.tensor_copy(
                xprevh[:, c:c + 1], xn1t[:, (c + 1) * TT - 1:(c + 1) * TT]),
                reads=["xn1t"], writes=["xprevh"])
            p.emit("vector", lambda v, c=c: v.tensor_sub(
                yt[:, c * TT:(c + 1) * TT], xn1t[:, c * TT:(c + 1) * TT], tmpd[:]),
                reads=["xn1t", "tmpd"], writes=["yt"])
            p.emit("vector", lambda v, c=c: v.scalar_tensor_tensor(
                kact[:, c * TT:(c + 1) * TT], yt[:, c * TT:(c + 1) * TT],
                mukc[:, c:c + 1], tmpd[:], ALU.mult, ALU.add),
                reads=["yt", "mukc", "tmpd"], writes=["kact"])
            p.emit("vector", lambda v, c=c: v.scalar_tensor_tensor(
                yt[:, c * TT:(c + 1) * TT], yt[:, c * TT:(c + 1) * TT],
                murc[:, c:c + 1], tmpd[:], ALU.mult, ALU.add),
                reads=["yt", "murc", "tmpd"], writes=["yt"])
        for m in range(DC):
            bank, bkey = (pe_a, "pe_a") if m % 2 == 0 else (pe_b, "pe_b")
            stream_mm(wrT_in, bank, bkey, yt, "yt", m)
            p.emit("scalar", lambda s, m=m, bank=bank: s.activation(
                rbuf[:, m * TT:(m + 1) * TT], bank[:], AF.Sigmoid),
                reads=[bkey], writes=["rbuf"])
        for m in range(DC):
            bank, bkey = (pe_a, "pe_a") if m % 2 == 0 else (pe_b, "pe_b")
            stream_mm(wkT_in, bank, bkey, kact, "kact", m)
            p.emit("scalar", lambda s, m=m, bank=bank: s.activation(
                yt[:, m * TT:(m + 1) * TT], bank[:], AF.Relu),
                reads=[bkey], writes=["yt"])
            p.emit("scalar", lambda s, m=m: s.activation(
                yt[:, m * TT:(m + 1) * TT], yt[:, m * TT:(m + 1) * TT], AF.Square),
                reads=["yt"], writes=["yt"])
        for m in range(DC):
            bank, bkey = (pe_a, "pe_a") if m % 2 == 0 else (pe_b, "pe_b")
            stream_mm(wvT_in, bank, bkey, yt, "yt", m)
            p.emit("scalar", lambda s, m=m, bank=bank: s.copy(
                kact[:, m * TT:(m + 1) * TT], bank[:]),
                reads=[bkey], writes=["kact"])
            p.emit("vector", lambda v, m=m: v.tensor_mul(
                kact[:, m * TT:(m + 1) * TT], kact[:, m * TT:(m + 1) * TT],
                rbuf[:, m * TT:(m + 1) * TT]),
                reads=["kact", "rbuf"], writes=["kact"])
            p.emit("vector", lambda v, m=m: v.tensor_add(
                x1t[:, m * TT:(m + 1) * TT], x1t[:, m * TT:(m + 1) * TT],
                kact[:, m * TT:(m + 1) * TT]),
                reads=["x1t", "kact"], writes=["x1t"])
        p.dma("sync", out_t.rearrange("(c p) t -> p c t", p=128)[:, :, t0:t0 + TT],
              x1t[:].rearrange("p (c t) -> p c t", t=TT), reads=["x1t"])

    p.final_dma_wait()
    p.finalize()
    return nc


def _prep_host(inputs):
    g = {}
    x = np.ascontiguousarray(inputs["L0_attn_x"])
    g["x_t_cores"] = [np.ascontiguousarray(x[b].T) for b in range(8)]

    def col(v):
        return np.ascontiguousarray(np.asarray(v).reshape(-1, 1))

    def chunked(v):
        return np.ascontiguousarray(np.asarray(v).reshape(DC, 128).T)

    g["norm_w"] = chunked(inputs["L0_attn_norm_w"])
    g["norm_b"] = chunked(inputs["L0_attn_norm_b"])
    for pre in ("fwd", "bwd"):
        P_ = f"L0_attn_{pre}_"
        g[f"{pre}_ipT"] = np.ascontiguousarray(inputs[P_ + "in_proj_w"].T)
        g[f"{pre}_opT"] = np.ascontiguousarray(inputs[P_ + "out_proj_w"].T)
        g[f"{pre}_xpT"] = np.ascontiguousarray(inputs[P_ + "x_proj_w"].T)
        g[f"{pre}_dtT"] = np.ascontiguousarray(inputs[P_ + "dt_proj_w"].T)
        cw = inputs[P_ + "conv_w"][:, 0, :]
        g[f"{pre}_cw"] = np.ascontiguousarray(
            cw.reshape(DC, 128, 4).transpose(1, 0, 2).reshape(128, DC * 4))
        g[f"{pre}_cb"] = chunked(inputs[P_ + "conv_b"])
        g[f"{pre}_dtb"] = chunked(inputs[P_ + "dt_proj_b"])
        al = inputs[P_ + "A_log"]
        g[f"{pre}_alog"] = np.ascontiguousarray(
            al.reshape(DC, 128, 16).transpose(1, 0, 2).reshape(128, DC * 16))
        g[f"{pre}_Dv"] = chunked(inputs[P_ + "D"])
    g["w_fc"] = np.ascontiguousarray(inputs["L0_ffn_w_fc"])
    g["w_proj"] = np.ascontiguousarray(inputs["L0_ffn_w_proj"])
    g["ffn_uN"] = np.ascontiguousarray(inputs["L0_ffn_u"])
    g["ffn_vT"] = np.ascontiguousarray(inputs["L0_ffn_v"].T)
    g["ffn_fbT"] = np.ascontiguousarray(inputs["L0_ffn_fast_b"].T)
    g["ffn_gate"] = np.ascontiguousarray(np.asarray(inputs["L0_ffn_gate"]).reshape(1, 1))
    A = np.asarray(inputs["L1_attn_A"])
    Ag = A.reshape(128, 8, 16, 16)
    xblk = np.zeros((128, 128, 128), np.float32)
    zblk = np.zeros((128, 128, 128), np.float32)
    for dlt in range(8):
        sl = slice(dlt * 16, dlt * 16 + 16)
        xblk[:, sl, sl] = Ag[:, dlt].transpose(0, 2, 1)
        zblk[:, sl, sl] = Ag[:, dlt]
    g["l1_xblk"] = np.ascontiguousarray(xblk.transpose(1, 0, 2).reshape(128, 128 * 128))
    g["l1_zblk"] = np.ascontiguousarray(zblk.transpose(1, 0, 2).reshape(128, 128 * 128))
    Bm = np.asarray(inputs["L1_attn_Bm"]).reshape(128, 8, 16)
    Cm = np.asarray(inputs["L1_attn_Cm"]).reshape(128, 8, 16)
    bce = np.zeros((128, 128, 128), np.float32)
    cce = np.zeros((128, 128, 128), np.float32)
    for dlt in range(8):
        for n_ in range(16):
            bce[:, dlt * 16 + n_, dlt * 16 + 15] = Bm[:, dlt, n_]
            cce[:, dlt * 16 + n_, dlt * 16 + 0] = Cm[:, dlt, n_]
    g["l1_bce"] = np.ascontiguousarray(bce.transpose(1, 0, 2).reshape(128, 128 * 128))
    g["l1_cce"] = np.ascontiguousarray(cce.transpose(1, 0, 2).reshape(128, 128 * 128))
    obd = np.zeros((128, 8), np.float32)
    for dlt in range(8):
        obd[dlt * 16:dlt * 16 + 16, dlt] = 1.0
    g["l1_obd"] = obd
    g["ident"] = np.eye(128, dtype=np.float32)
    g["l1_lnw"] = chunked(inputs["L1_attn_ln_w"])
    g["l1_lnb"] = chunked(inputs["L1_attn_ln_b"])
    g["l1_dm"] = chunked(inputs["L1_attn_Dm"])
    g["l1_woT"] = np.ascontiguousarray(inputs["L1_attn_W_out"].T)
    g["l1_bo"] = chunked(inputs["L1_attn_b_out"])
    g["l1_cc"] = np.ascontiguousarray(Cm.transpose(1, 2, 0).reshape(128, 128))
    g["l1_wrT"] = np.ascontiguousarray(inputs["L1_ffn_W_r"].T)
    g["l1_wkT"] = np.ascontiguousarray(inputs["L1_ffn_W_k"].T)
    g["l1_wvT"] = np.ascontiguousarray(inputs["L1_ffn_W_v"].T)
    g["l1_mur"] = chunked(inputs["L1_ffn_mu_r"])
    g["l1_muk"] = chunked(inputs["L1_ffn_mu_k"])
    return g


def kernel(**inputs):
    g = _prep_host(inputs)
    nc = build_nc("full")
    in_maps = []
    for b in range(8):
        m = {k: v for k, v in g.items() if k != "x_t_cores"}
        m["x_t"] = g["x_t_cores"][b]
        in_maps.append(m)
    res = run_bass_kernel_spmd(nc, in_maps, core_ids=list(range(8)))
    out = np.empty((8, T, D), np.float32)
    for b in range(8):
        out[b] = res.results[b]["out_t"].T
    return out
